# version 02 (final)
# baseline (speedup 1.0000x reference)
import numpy as np
from contextlib import ExitStack

import concourse.bass as bass
import concourse.mybir as mybir
from concourse.bass_utils import run_bass_kernel_spmd

F32 = mybir.dt.float32
BF16 = mybir.dt.bfloat16
AF = mybir.ActivationFunctionType
ALU = mybir.AluOpType
AX = mybir.AxisListType

NCORE = 8
D = 1024
NCH = 8
T = 512
NQ = 4
NST = 8
SEQ = 2048
LN_EPS = 1e-5
WKV_EPS = 1e-8

ENG2ATTR = {"t": "tensor", "v": "vector", "s": "scalar", "q": "sync", "g": "gpsimd"}


class Prog:

    def __init__(self, nc):
        self.nc = nc
        self.ev = {e: [] for e in ENG2ATTR}
        self.cnt = {}
        self.sem_names = []

    def _sem(self, name):
        if name not in self.cnt:
            self.cnt[name] = 0
            self.sem_names.append(name)
        return name

    def emit(self, eng, fn, inc=None, inc_n=1):
        tok = None
        if inc is not None:
            self._sem(inc)
            self.cnt[inc] += inc_n
            tok = (inc, self.cnt[inc])
        self.ev[eng].append((fn, inc, inc_n))
        return tok

    def dma(self, eng, out, in_, inc, **kw):
        return self.emit(eng, lambda e: e.dma_start(out=out, in_=in_, **kw), inc=inc, inc_n=16)

    def wait(self, eng, tok):
        seen = {}

        def walk(t):
            if t is None:
                return
            if isinstance(t, list):
                for u in t:
                    walk(u)
            else:
                s, v = t
                seen[s] = max(seen.get(s, 0), v)

        walk(tok)
        for s, v in seen.items():
            self._sem(s)
            self.ev[eng].append((("wait", s, v), None, None))

    def barrier(self, engines="tvs"):
        toks = {}
        b = self.bscr
        for e in engines:
            if e == "s":
                self.emit("s", lambda ee: ee.activation(b[0:1, 2:3], b[0:1, 0:1], AF.Copy))
                toks[e] = self.emit("s", lambda ee: ee.activation(
                    b[0:1, 2:3], b[0:1, 0:1], AF.Copy), inc="bars")
            elif e == "v":
                self.emit("v", lambda ee: ee.tensor_copy(b[0:1, 3:4], b[0:1, 1:2]))
                toks[e] = self.emit("v", lambda ee: ee.tensor_copy(
                    b[0:1, 3:4], b[0:1, 1:2]), inc="barv")
            else:
                toks[e] = self.emit(e, (lambda ee: ee.nop()), inc="bar" + e)
        for e in engines:
            self.wait(e, [toks[o] for o in engines if o != e])
        return [toks[e] for e in engines]

    def realize(self, es):
        sems = {n: es.enter_context(self.nc.semaphore(n)) for n in self.sem_names}
        block = es.enter_context(self.nc.Block())

        def mk(eng_key):
            events = self.ev[eng_key]

            def body(e):
                for item, inc, inc_n in events:
                    if isinstance(item, tuple) and item and item[0] == "wait":
                        e.wait_ge(sems[item[1]], item[2])
                    else:
                        ins = item(e)
                        if inc is not None:
                            ins.then_inc(sems[inc], inc_n)

            return body

        for key, attr in ENG2ATTR.items():
            getattr(block, attr)(mk(key))



def row_(v):
    return np.ascontiguousarray(np.asarray(v, dtype=np.float32).reshape(1, -1))


def col_(v):
    return np.ascontiguousarray(np.asarray(v, dtype=np.float32).reshape(NCH, 128).T)


def ctr_(w):
    return np.ascontiguousarray(np.asarray(w, dtype=np.float32).T)



def emit_ln_rows(p, E, x_aps, tok_x):
    p.wait("s", tok_x)
    p.wait("t", tok_x)
    p.wait("t", E["stats_free"][0])
    t_stat = None
    mmsq_toks = E["xsq_free"]
    for c in range(NCH):
        cp = c % 2
        xc = x_aps[c]
        xsqc = E["xsq"][:, cp, :]
        p.wait("s", mmsq_toks[cp])
        t_sq = p.emit("s", lambda s, xc=xc, xsqc=xsqc: s.activation(
            xsqc, xc, AF.Square), inc="ssq")
        p.wait("t", t_sq)
        p.emit("t", lambda t, xc=xc, c=c: t.matmul(
            E["ps_s1"][:, :], E["ones_col"][:, :], xc,
            start=(c == 0), stop=(c == NCH - 1)), inc=None)
        t_stat = p.emit("t", lambda t, xsqc=xsqc, c=c: t.matmul(
            E["ps_s2"][:, :], E["ones_col"][:, :], xsqc,
            start=(c == 0), stop=(c == NCH - 1)), inc="mmsq")
        mmsq_toks[cp] = t_stat
    p.wait("v", t_stat)
    p.emit("v", lambda v: v.tensor_copy(E["s1r"][:], E["ps_s1"][:, :]), inc=None)
    p.emit("v", lambda v: v.scalar_tensor_tensor(
        E["var_r"][:], E["s1r"][:], -1.0 / D, E["s1r"][:], ALU.mult, ALU.mult), inc=None)
    t_var = p.emit("v", lambda v: v.tensor_tensor(
        E["var_r"][:], E["var_r"][:], E["ps_s2"][:, :], ALU.add), inc="vr")
    E["stats_free"][0] = t_var
    p.wait("s", t_var)
    p.emit("s", lambda s: s.activation(
        E["lnv_r"][:], E["var_r"][:], AF.Ln, bias=E["eps1"][:, :], scale=1.0 / D), inc=None)
    t_a = p.emit("s", lambda s: s.activation(
        E["a_row"][:], E["lnv_r"][:], AF.Exp, scale=-0.5), inc="sr")
    p.wait("v", t_a)
    t_b = p.emit("v", lambda v: v.scalar_tensor_tensor(
        E["b_row2"][:], E["s1r"][:], -1.0 / D, E["a_row"][:], ALU.mult, ALU.mult), inc="vr")
    return [t_a, t_b]


def emit_ln_apply_chunk(p, E, c, x_ap, out_ap, tok_rows, g_row, b_row, tok_out_free):
    cp = c % 2
    p.wait("t", tok_rows)
    p.wait("t", E["ab_free"][cp])
    gsl_ap = E["ones_bcol"][0:1, :] if g_row is None else g_row[0:1, c * 128:(c + 1) * 128]
    p.emit("t", lambda t, cp=cp, gsl_ap=gsl_ap: t.matmul(
        E["ps_ab"][:, cp, :], gsl_ap, E["a_row"][:, :], start=True, stop=True), inc=None)
    t_ab = p.emit("t", lambda t, cp=cp, gsl_ap=gsl_ap: t.matmul(
        E["ps_bb"][:, cp, :], gsl_ap, E["b_row2"][:, :], start=True,
        stop=(b_row is None)), inc=("mmab" if b_row is None else None))
    if b_row is not None:
        bsl_ap = b_row[0:1, c * 128:(c + 1) * 128]
        t_ab = p.emit("t", lambda t, cp=cp, bsl_ap=bsl_ap: t.matmul(
            E["ps_bb"][:, cp, :], bsl_ap, E["ones_brow"][:, :],
            start=False, stop=True), inc="mmab")
    p.wait("v", t_ab)
    p.wait("v", tok_out_free)
    p.emit("v", lambda v, cp=cp, x_ap=x_ap: v.tensor_tensor(
        E["lntmp"][:, cp, :], x_ap, E["ps_ab"][:, cp, :], ALU.mult), inc=None)
    t_xn = p.emit("v", lambda v, cp=cp, out_ap=out_ap: v.tensor_tensor(
        out_ap, E["lntmp"][:, cp, :], E["ps_bb"][:, cp, :], ALU.add), inc="vap")
    E["ab_free"][cp] = t_xn
    return t_xn


def alloc_ln_env(p, nc, es, sb):
    E = {}
    E["xsq"] = sb("ln_xsq", [128, 2, T])
    E["lntmp"] = sb("ln_tmp", [128, 2, T])
    E["s1r"] = sb("ln_s1r", [1, T])
    E["var_r"] = sb("ln_var", [1, T])
    E["lnv_r"] = sb("ln_lnv", [1, T])
    E["a_row"] = sb("ln_arow", [1, T], BF16)
    E["b_row2"] = sb("ln_brow", [1, T], BF16)
    E["ones_col"] = sb("ln_ones_col", [128, 1])
    E["ones_brow"] = sb("ln_ones_brow", [1, T], BF16)
    E["ones_bcol"] = sb("ln_ones_bcol", [1, 128], BF16)
    E["ps_s1"] = es.enter_context(nc.psum_tensor("ps_s1", [1, T], F32))
    E["ps_s2"] = es.enter_context(nc.psum_tensor("ps_s2", [1, T], F32))
    E["ps_ab"] = es.enter_context(nc.psum_tensor("ps_ab", [128, 2, T], F32))
    E["ps_bb"] = es.enter_context(nc.psum_tensor("ps_bb", [128, 2, T], F32))
    E["eps1"] = sb("ln_eps1", [1, 1])
    E["stats_free"] = [None]
    E["xsq_free"] = [None, None]
    E["ab_free"] = [None, None]
    t0 = p.emit("v", lambda v: v.memset(E["ones_col"][:], 1.0), inc="vr")
    p.emit("v", lambda v: v.memset(E["eps1"][:], LN_EPS), inc="vr")
    t1 = p.emit("v", lambda v: v.memset(E["ones_brow"][:], 1.0), inc="vr")
    t2 = p.emit("v", lambda v: v.memset(E["ones_bcol"][:], 1.0), inc="vr")
    return E, [t0, t1, t2]



def build_la1():
    nc = bass.Bass("TRN2", target_bir_lowering=False)
    x_d = nc.dram_tensor("x", [2, D, SEQ], F32, kind="ExternalInput").ap()
    wT_d = nc.dram_tensor("wT", [4, D, D], F32, kind="ExternalInput").ap()
    g_row_d = nc.dram_tensor("ln1g_row", [1, D], F32, kind="ExternalInput").ap()
    b_row_d = nc.dram_tensor("ln1b_row", [1, D], F32, kind="ExternalInput").ap()
    td_col_d = nc.dram_tensor("td_col", [128, NCH], F32, kind="ExternalInput").ap()
    td4_d = nc.dram_tensor("td4", [128, 32], F32, kind="ExternalInput").ap()
    esct_d = nc.dram_tensor("esct", [8, 128, 32], F32, kind="ExternalInput").ap()
    maskt_d = nc.dram_tensor("maskt", [8, 128, 32], F32, kind="ExternalInput").ap()

    num_d = nc.dram_tensor("num_o", [NST, D, T], F32, kind="ExternalOutput").ap()
    den_d = nc.dram_tensor("den_o", [NST, D, T], F32, kind="ExternalOutput").ap()
    er_d = nc.dram_tensor("er_o", [NST, D, T], F32, kind="ExternalOutput").ap()
    c_d = nc.dram_tensor("c_o", [4, 128, NCH], F32, kind="ExternalOutput").ap()

    cc_in = nc.dram_tensor("cc_in", [4, 128, NCH], F32, kind="Internal").ap()
    cc_out = nc.dram_tensor("cc_out", [32, 128, NCH], F32, kind="Internal").ap()

    p = Prog(nc)
    with ExitStack() as es:
        sb = lambda nm, shp, dt=F32: es.enter_context(nc.sbuf_tensor(nm, shp, dt))

        wsb = sb("wsb", [128, 4, NCH, D], BF16)
        xfm = sb("xfm", [128, 2, NCH, T])
        xnbf = sb("xnbf", [128, 2, NCH, T], BF16)
        dec_t = sb("dec_t", [128, NCH, T])
        ones_t = sb("ones_t", [128, T])
        SLOTS = ["er", "eww", "k", "v", "ewk", "gnum", "num", "den", "tmp"]
        slot = {nm: sb("sl_" + nm, [128, 2, T]) for nm in SLOTS}
        g_row = sb("g_row", [1, D], BF16)
        b_row = sb("b_row", [1, D], BF16)
        td_col = sb("td_col_s", [128, NCH])
        td4 = sb("td4_s", [128, 32])
        edec_col = sb("edec_col", [128, NCH])
        carry_n = sb("carry_n", [128, NCH])
        carry_dn = sb("carry_dn", [128, NCH])
        agg = sb("agg", [128, 8, 4, 8])
        cmb = sb("cmb", [128, 32])
        esct = sb("esct_s", [128, 8, 32])
        maskt = sb("maskt_s", [128, 8, 32])
        cprod = sb("cprod", [128, 32])
        cw = sb("cwt", [128, 32])

        E, tok_lnprep = alloc_ln_env(p, nc, es, sb)
        ps_p = es.enter_context(nc.psum_tensor("ps_p", [128, 2, T], F32))

        tok_w = None
        for j in range(4):
            tok_w = p.dma("g", wsb[:, j, :, :],
                          wT_d[j].rearrange("(k p) e -> p k e", p=128), inc="ldw")
        p.dma("g", g_row[:], g_row_d, inc="ldw")
        tok_gb = p.dma("g", b_row[:], b_row_d, inc="ldw")
        tok_td = p.dma("q", td_col[:], td_col_d, inc="ldq")
        p.dma("q", td4[:], td4_d, inc="ldq")
        p.dma("q", esct[:], esct_d.rearrange("j p c -> p j c"), inc="ldq")
        tok_cmbin = p.dma("q", maskt[:], maskt_d.rearrange("j p c -> p j c"), inc="ldq")

        t_ones = p.emit("v", lambda v: v.memset(ones_t[:], 1.0), inc="vr")
        p.emit("v", lambda v: v.memset(carry_n[:], 0.0), inc="vr")
        t_car0 = p.emit("v", lambda v: v.memset(carry_dn[:], 0.0), inc="vr")
        p.wait("s", tok_td)
        t_edec = p.emit("s", lambda s: s.activation(edec_col[:], td_col[:], AF.Exp), inc="sr")
        p.wait("v", [t_edec, t_ones])
        t_dec = None
        for c in range(NCH):
            t_dec = p.emit("v", lambda v, c=c: v.tensor_scalar(
                dec_t[:, c, :], ones_t[:], edec_col[:, c:c + 1], None, ALU.mult), inc="vr")

        x_free = [None, None]
        xn_free = [None, None]
        slot_free = {nm: [None, None] for nm in SLOTS}
        psp_free = [None, None]
        agg_staged = None

        for st in range(NST):
            b, q = st // NQ, st % NQ
            par = st % 2

            p.wait("q", x_free[par])
            t_x = p.dma("q", xfm[:, par, :, :],
                        x_d[b].rearrange("(k p) t -> p k t", p=128)[:, :, q * T:(q + 1) * T],
                        inc="ldx")

            x_aps = [xfm[:, par, c, :] for c in range(NCH)]
            tok_rows = emit_ln_rows(p, E, x_aps, t_x)

            p.wait("t", tok_gb)
            p.wait("v", tok_lnprep if st == 0 else None)
            t_xn_last = None
            for c in range(NCH):
                t_xn_last = emit_ln_apply_chunk(
                    p, E, c, xfm[:, par, c, :], xnbf[:, par, c, :], tok_rows,
                    g_row, b_row, xn_free[par] if c == 0 else None)
            x_free[par] = [t_xn_last, tok_rows[0], tok_rows[1]]

            p.wait("t", tok_w if st == 0 else None)
            p.wait("t", t_xn_last)
            for ec in range(NCH):
                ep = ec % 2
                evtok = {}
                for j, nm in enumerate(["er", "eww", "k", "v"]):
                    p.wait("t", psp_free[ep])
                    tok_mm = None
                    for dk in range(NCH):
                        tok_mm = p.emit("t", lambda t, j=j, dk=dk, ec=ec, ep=ep, par=par: t.matmul(
                            ps_p[:, ep, :], wsb[:, j, dk, ec * 128:(ec + 1) * 128],
                            xnbf[:, par, dk, :],
                            start=(dk == 0), stop=(dk == NCH - 1)),
                            inc=("mmp" if dk == NCH - 1 else None))
                    p.wait("s", tok_mm)
                    p.wait("s", slot_free[nm][ep])
                    if nm == "er":
                        t_ev = p.emit("s", lambda s, ep=ep: s.activation(
                            slot["er"][:, ep, :], ps_p[:, ep, :], AF.Exp, scale=-1.0), inc="se")
                    elif nm == "eww":
                        t_ev = p.emit("s", lambda s, ep=ep: s.activation(
                            slot["eww"][:, ep, :], ps_p[:, ep, :], AF.Exp), inc="se")
                    else:
                        t_ev = p.emit("s", lambda s, nm=nm, ep=ep: s.copy(
                            slot[nm][:, ep, :], ps_p[:, ep, :]), inc="se")
                    psp_free[ep] = t_ev
                    evtok[nm] = t_ev

                p.wait("v", [evtok["eww"], evtok["k"], slot_free["tmp"][ep]])
                t_dif = p.emit("v", lambda v, ep=ep: v.tensor_tensor(
                    slot["tmp"][:, ep, :], slot["k"][:, ep, :], slot["eww"][:, ep, :],
                    ALU.subtract), inc="vsl")
                p.wait("s", [t_dif, slot_free["ewk"][ep]])
                t_ewk = p.emit("s", lambda s, ep=ep: s.activation(
                    slot["ewk"][:, ep, :], slot["tmp"][:, ep, :], AF.Exp), inc="se")
                p.wait("v", [t_ewk, evtok["v"], slot_free["gnum"][ep]])
                t_gn = p.emit("v", lambda v, ep=ep: v.tensor_tensor(
                    slot["gnum"][:, ep, :], slot["ewk"][:, ep, :], slot["v"][:, ep, :],
                    ALU.mult), inc="vsl")
                p.wait("v", [t_dec, slot_free["num"][ep], slot_free["den"][ep]])
                t_sn = p.emit("v", lambda v, ec=ec, ep=ep: v.tensor_tensor_scan(
                    slot["num"][:, ep, :], dec_t[:, ec, :], slot["gnum"][:, ep, :],
                    carry_n[:, ec:ec + 1], ALU.mult, ALU.add), inc="vsl")
                t_sd = p.emit("v", lambda v, ec=ec, ep=ep: v.tensor_tensor_scan(
                    slot["den"][:, ep, :], dec_t[:, ec, :], slot["ewk"][:, ep, :],
                    carry_dn[:, ec:ec + 1], ALU.mult, ALU.add), inc="vsl")
                p.emit("v", lambda v, ec=ec, ep=ep: v.tensor_copy(
                    carry_n[:, ec:ec + 1], slot["num"][:, ep, T - 1:T]), inc=None)
                t_cd = p.emit("v", lambda v, ec=ec, ep=ep: v.tensor_copy(
                    carry_dn[:, ec:ec + 1], slot["den"][:, ep, T - 1:T]), inc="vsl")

                p.wait("q", [t_sn, t_sd, evtok["er"]])
                p.dma("q", num_d[st, ec * 128:(ec + 1) * 128, :], slot["num"][:, ep, :], inc="stq")
                p.dma("q", den_d[st, ec * 128:(ec + 1) * 128, :], slot["den"][:, ep, :], inc="stq")
                t_sp = p.dma("q", er_d[st, ec * 128:(ec + 1) * 128, :], slot["er"][:, ep, :], inc="stq")

                slot_free["er"][ep] = t_sp
                slot_free["eww"][ep] = t_dif
                slot_free["k"][ep] = t_dif
                slot_free["v"][ep] = t_gn
                slot_free["ewk"][ep] = t_sd
                slot_free["gnum"][ep] = t_sn
                slot_free["num"][ep] = [t_sp, t_cd]
                slot_free["den"][ep] = [t_sp, t_cd]
                slot_free["tmp"][ep] = t_ewk
            xn_free[par] = tok_mm

            if q == NQ - 1:
                p.wait("q", t_cd)
                t_a1 = p.dma("q", cc_in[2 * b + 0], carry_n[:], inc="stq")
                t_a2 = p.dma("q", cc_in[2 * b + 1], carry_dn[:], inc="stq")
                if b == 0:
                    p.wait("v", t_a2)
                    p.emit("v", lambda v: v.memset(carry_n[:], 0.0), inc=None)
                    p.emit("v", lambda v: v.memset(carry_dn[:], 0.0), inc="vr")
                else:
                    agg_staged = t_a2

        p.wait("g", agg_staged)
        t_cc = p.emit("g", lambda g: g.collective_compute(
            "AllGather", ALU.bypass, replica_groups=[list(range(NCORE))],
            ins=[cc_in.opt()], outs=[cc_out.opt()]), inc="cc")
        p.wait("q", t_cc)
        t_ag = None
        for r in range(NCORE):
            t_ag = p.dma("q", agg[:, r],
                         cc_out[4 * r:4 * r + 4].rearrange("w p c -> p w c"), inc="ldq")
        p.wait("v", [t_ag, tok_cmbin])
        p.emit("v", lambda v: v.memset(cmb[:], 0.0), inc=None)
        for j in range(NCORE):
            t_pr = p.emit("v", lambda v, j=j: v.tensor_tensor(
                cprod[:], td4[:], esct[:, j, :], ALU.mult), inc="vcm")
            p.wait("s", t_pr)
            t_ex = p.emit("s", lambda s: s.activation(cprod[:], cprod[:], AF.Exp), inc="scm")
            p.wait("v", t_ex)
            p.emit("v", lambda v, j=j: v.tensor_tensor(cw[:], cprod[:], agg[:, j].rearrange("p w c -> p (w c)"), ALU.mult), inc=None)
            p.emit("v", lambda v, j=j: v.tensor_tensor(cw[:], cw[:], maskt[:, j, :], ALU.mult), inc=None)
            t_cmb = p.emit("v", lambda v: v.tensor_tensor(cmb[:], cmb[:], cw[:], ALU.add), inc="vcm")
        p.wait("q", t_cmb)
        t_out = p.dma("q", c_d.rearrange("w p c -> p w c"), cmb[:].rearrange("p (w c) -> p w c", w=4), inc="stq")
        p.wait("q", t_out)

        p.realize(es)
    return nc



def build_la2():
    nc = bass.Bass("TRN2", target_bir_lowering=False)
    x_d = nc.dram_tensor("x", [2, D, SEQ], F32, kind="ExternalInput").ap()
    num_d = nc.dram_tensor("num_i", [NST, D, T], F32, kind="ExternalInput").ap()
    den_d = nc.dram_tensor("den_i", [NST, D, T], F32, kind="ExternalInput").ap()
    er_d = nc.dram_tensor("er_i", [NST, D, T], F32, kind="ExternalInput").ap()
    c_dram = nc.dram_tensor("c_i", [4, 128, NCH], F32, kind="ExternalInput").ap()
    woT_d = nc.dram_tensor("woT", [D, D], F32, kind="ExternalInput").ap()
    wrecT_d = nc.dram_tensor("wrecT", [D, D], F32, kind="ExternalInput").ap()
    g2_row_d = nc.dram_tensor("ln2g_row", [1, D], F32, kind="ExternalInput").ap()
    b2_row_d = nc.dram_tensor("ln2b_row", [1, D], F32, kind="ExternalInput").ap()
    td_row_d = nc.dram_tensor("td_row", [1, D], F32, kind="ExternalInput").ap()
    iota1_d = nc.dram_tensor("iota1", [1, T], F32, kind="ExternalInput").ap()

    u_d = nc.dram_tensor("u_o", [NST, D, T], F32, kind="ExternalOutput").ap()
    xn2_d = nc.dram_tensor("xn2_o", [NST, D, T], BF16, kind="ExternalOutput").ap()
    erec_d = nc.dram_tensor("erec_o", [NST, D, T], F32, kind="ExternalOutput").ap()

    p = Prog(nc)
    with ExitStack() as es:
        sb = lambda nm, shp, dt=F32: es.enter_context(nc.sbuf_tensor(nm, shp, dt))

        woT = sb("woT_s", [128, NCH, D], BF16)
        wrecT = sb("wrecT_s", [128, NCH, D], BF16)
        xfm = sb("xfm", [128, NCH, T])
        num_s = sb("num_s", [128, NCH, T])
        den_s = sb("den_s", [128, NCH, T])
        er_s = sb("er_s", [128, NCH, T])
        pw = sb("pw", [128, 2, T])
        rwkv = sb("rwkv", [128, NCH, T], BF16)
        u_s = sb("u_s", [128, NCH, T])
        xn2 = sb("xn2", [128, NCH, T], BF16)
        erec = sb("erec", [128, NCH, T])
        csb = sb("csb", [128, 4, NCH])
        td_row = sb("td_row_s", [1, D])
        iota1 = sb("iota1_s", [1, T])
        q_row = sb("q_row", [1, T])
        e1t = sb("e1t", [128, 2, T])
        dent = sb("dent", [128, 2, T])
        rect = sb("rect", [128, 2, T])
        eps8 = sb("eps8", [1, 1])

        E, tok_lnprep = alloc_ln_env(p, nc, es, sb)
        g2_row = sb("g2_row", [1, D], BF16)
        b2_row = sb("b2_row", [1, D], BF16)
        ps_p = es.enter_context(nc.psum_tensor("ps_p", [128, 2, T], F32))

        tok_w = p.dma("g", woT[:], woT_d.rearrange("(k p) e -> p k e", p=128), inc="ldw")
        tok_w2 = p.dma("g", wrecT[:], wrecT_d.rearrange("(k p) e -> p k e", p=128), inc="ldw")
        p.dma("g", g2_row[:], g2_row_d, inc="ldw")
        tok_gb = p.dma("g", b2_row[:], b2_row_d, inc="ldw")
        p.dma("q", td_row[:], td_row_d, inc="ldq")
        tok_iota = p.dma("q", iota1[:], iota1_d, inc="ldq")
        tok_c = p.dma("q", csb[:], c_dram.rearrange("w p c -> p w c"), inc="ldq")
        p.emit("v", lambda v: v.memset(eps8[:], WKV_EPS), inc="vr")

        qrow_free = [None]
        big_free = [None]
        u_free = [None]
        rwkv_free = [None]
        erec_free = [None]
        x_free = [None]
        xn2_free = [None]
        pw_free = [None, None]
        sl_free = {nm: [None, None] for nm in ["e1t", "dent", "rect"]}
        psp_free = [None, None]

        for st in range(NST):
            b, q = st // NQ, st % NQ
            par = st % 2

            p.wait("q", x_free[0])
            t_x = p.dma("q", xfm[:, :, :],
                        x_d[b].rearrange("(k p) t -> p k t", p=128)[:, :, q * T:(q + 1) * T],
                        inc="ldx")
            p.wait("q", big_free[0])
            t_num = p.dma("q", num_s[:], num_d[st].rearrange("(k p) t -> p k t", p=128), inc="ldx")
            t_den = p.dma("q", den_s[:], den_d[st].rearrange("(k p) t -> p k t", p=128), inc="ldx")
            t_er = p.dma("q", er_s[:], er_d[st].rearrange("(k p) t -> p k t", p=128), inc="ldx")

            p.wait("v", tok_iota if st == 0 else qrow_free[0])
            t_qr = p.emit("v", lambda v, q=q: v.tensor_scalar(
                q_row[:], iota1[:], float(T * q), None, ALU.add), inc="vr")

            t_rw_last = None
            fix_toks = []
            for c in range(NCH):
                cp = c % 2
                tdsl = td_row[0:1, c * 128:(c + 1) * 128]
                p.wait("t", [E["ab_free"][cp], t_qr, tok_c])
                t_r1 = p.emit("t", lambda t, cp=cp, tdsl=tdsl: t.matmul(
                    E["ps_ab"][:, cp, :], tdsl, q_row[:, :], start=True, stop=True), inc="mmpw")
                p.wait("s", [t_r1, pw_free[cp]])
                t_pw = p.emit("s", lambda s, cp=cp: s.activation(
                    pw[:, cp, :], E["ps_ab"][:, cp, :], AF.Exp), inc="se")
                E["ab_free"][cp] = t_pw
                p.wait("v", [t_pw, t_num, t_den, t_er])
                t_fn = p.emit("v", lambda v, c=c, cp=cp, b=b: v.scalar_tensor_tensor(
                    num_s[:, c, :], pw[:, cp, :], csb[:, 2 * b + 0, c:c + 1],
                    num_s[:, c, :], ALU.mult, ALU.add), inc="vf")
                t_fd = p.emit("v", lambda v, c=c, cp=cp, b=b: v.scalar_tensor_tensor(
                    den_s[:, c, :], pw[:, cp, :], csb[:, 2 * b + 1, c:c + 1],
                    den_s[:, c, :], ALU.mult, ALU.add), inc="vf")
                pw_free[cp] = [t_fn, t_fd]
                t_e1 = p.emit("v", lambda v, c=c, cp=cp: v.tensor_scalar(
                    e1t[:, cp, :], er_s[:, c, :], 1.0, None, ALU.add), inc="vf")
                p.wait("v", sl_free["dent"][cp])
                t_d3 = p.emit("v", lambda v, c=c, cp=cp: v.scalar_tensor_tensor(
                    dent[:, cp, :], den_s[:, c, :], WKV_EPS, e1t[:, cp, :],
                    ALU.add, ALU.mult), inc="vf")
                p.wait("s", [t_d3, sl_free["rect"][cp]])
                p.emit("s", lambda s, cp=cp: s.activation(
                    dent[:, cp, :], dent[:, cp, :], AF.Ln), inc=None)
                t_rc = p.emit("s", lambda s, cp=cp: s.activation(
                    rect[:, cp, :], dent[:, cp, :], AF.Exp, scale=-1.0), inc="se")
                p.wait("v", [t_rc, rwkv_free[0] if c == 0 else None])
                t_rw = p.emit("v", lambda v, c=c, cp=cp: v.tensor_tensor(
                    rwkv[:, c, :], num_s[:, c, :], rect[:, cp, :], ALU.mult), inc="vf")
                qrow_free[0] = t_r1
                sl_free["e1t"][cp] = t_d3
                sl_free["dent"][cp] = t_rc
                sl_free["rect"][cp] = t_rw
                t_rw_last = t_rw
                fix_toks += [t_fn, t_fd, t_rw]

            p.wait("t", [tok_w, t_rw_last])
            u_toks = []
            wo_mm = None
            for ec in range(NCH):
                ep = ec % 2
                p.wait("t", psp_free[ep])
                for dk in range(NCH):
                    wo_mm = p.emit("t", lambda t, dk=dk, ec=ec, ep=ep: t.matmul(
                        ps_p[:, ep, :], woT[:, dk, ec * 128:(ec + 1) * 128],
                        rwkv[:, dk, :], start=(dk == 0), stop=(dk == NCH - 1)),
                        inc=("mmp" if dk == NCH - 1 else None))
                p.wait("v", [wo_mm, t_x, u_free[0]] if ec == 0 else [wo_mm])
                t_u = p.emit("v", lambda v, ec=ec, ep=ep: v.tensor_tensor(
                    u_s[:, ec, :], xfm[:, ec, :], ps_p[:, ep, :], ALU.add), inc="vf")
                psp_free[ep] = t_u
                u_toks.append(t_u)
                p.wait("q", t_u)
                p.dma("q", u_d[st, ec * 128:(ec + 1) * 128, :], u_s[:, ec, :], inc="stq")
            x_free[0] = u_toks[-1]
            rwkv_free[0] = wo_mm

            u_aps = [u_s[:, c, :] for c in range(NCH)]
            tok_rows = emit_ln_rows(p, E, u_aps, u_toks[-1])
            p.wait("t", tok_gb)
            p.wait("v", tok_lnprep if st == 0 else None)
            t_xn_last = None
            xn_sp = None
            for c in range(NCH):
                t_xn_last = emit_ln_apply_chunk(
                    p, E, c, u_s[:, c, :], xn2[:, c, :], tok_rows,
                    g2_row, b2_row, xn2_free[0] if c == 0 else None)
                p.wait("q", t_xn_last)
                xn_sp = p.dma("q", xn2_d[st, c * 128:(c + 1) * 128, :], xn2[:, c, :], inc="stq")

            p.wait("t", [tok_w2, t_xn_last])
            er_toks = []
            for ec in range(NCH):
                ep = ec % 2
                p.wait("t", psp_free[ep])
                wr_mm = None
                for dk in range(NCH):
                    wr_mm = p.emit("t", lambda t, dk=dk, ec=ec, ep=ep: t.matmul(
                        ps_p[:, ep, :], wrecT[:, dk, ec * 128:(ec + 1) * 128],
                        xn2[:, dk, :], start=(dk == 0), stop=(dk == NCH - 1)),
                        inc=("mmp" if dk == NCH - 1 else None))
                p.wait("s", [wr_mm, erec_free[0] if ec == 0 else None])
                t_erc = p.emit("s", lambda s, ec=ec, ep=ep: s.activation(
                    erec[:, ec, :], ps_p[:, ep, :], AF.Exp, scale=-1.0), inc="se")
                psp_free[ep] = t_erc
                er_toks.append(t_erc)
                p.wait("q", t_erc)
                t_q = p.dma("q", erec_d[st, ec * 128:(ec + 1) * 128, :], erec[:, ec, :], inc="stq")
            xn2_free[0] = [wr_mm, xn_sp]
            erec_free[0] = t_q
            big_free[0] = [t_rw_last, fix_toks[-1]]
            u_free[0] = [E["xsq_free"][0], E["xsq_free"][1], t_xn_last, xn_sp]

        p.wait("q", t_q)
        p.realize(es)
    return nc


def la2_in_maps(inp, la1_res):
    x = np.asarray(inp["L0_attn_x"], dtype=np.float32)
    woT = ctr_(inp["L0_attn_Wo"])
    wrecT = ctr_(inp["L0_attn_Wrec"])
    g2 = row_(inp["L0_attn_ln2_g"])
    b2 = row_(inp["L0_attn_ln2_b"])
    td_row = row_(inp["L0_attn_time_decay"])
    iota1 = np.arange(1, T + 1, dtype=np.float32).reshape(1, T)
    maps = []
    for c in range(NCORE):
        sl = slice(c * SEQ, (c + 1) * SEQ)
        xc = np.ascontiguousarray(np.transpose(x[:, sl, :], (0, 2, 1)))
        r = la1_res[c]
        maps.append(dict(x=xc, num_i=r["num_o"], den_i=r["den_o"], er_i=r["er_o"],
                         c_i=r["c_o"], woT=woT, wrecT=wrecT, ln2g_row=g2,
                         ln2b_row=b2, td_row=td_row, iota1=iota1))
    return maps


def run_la2(inp, la1_res):
    nc = _get_nc("la2", build_la2)
    maps = la2_in_maps(inp, la1_res)
    res = _run_spmd(nc, maps)
    return res.results



def build_la3():
    nc = bass.Bass("TRN2", target_bir_lowering=False)
    x_d = nc.dram_tensor("x", [2, D, SEQ], F32, kind="ExternalInput").ap()
    u_dr = nc.dram_tensor("u_i", [NST, D, T], F32, kind="ExternalInput").ap()
    xn2_d = nc.dram_tensor("xn2_i", [NST, D, T], BF16, kind="ExternalInput").ap()
    erec_d = nc.dram_tensor("erec_i", [NST, D, T], F32, kind="ExternalInput").ap()
    wkeyT_d = nc.dram_tensor("wkeyT", [D, 4096], F32, kind="ExternalInput").ap()
    wvalT_d = nc.dram_tensor("wvalT", [4096, D], F32, kind="ExternalInput").ap()
    x1_d = nc.dram_tensor("x1_o", [NST, D, T], F32, kind="ExternalOutput").ap()

    FCH = 32
    p = Prog(nc)
    with ExitStack() as es:
        sb = lambda nm, shp, dt=F32: es.enter_context(nc.sbuf_tensor(nm, shp, dt))

        wkeyT = sb("wkeyT_s", [128, NCH, 4096], BF16)
        wvalT = sb("wvalT_s", [128, FCH, D], BF16)
        xn2 = sb("xn2_s", [128, NCH, T], BF16)
        kk = sb("kk_s", [128, 8, T], BF16)
        vcm = sb("vcm_s", [128, NCH, T])
        a_s = sb("a_s", [128, NCH, T])
        sl_u = sb("sl_u", [128, 2, T])
        sl_x = sb("sl_x", [128, 2, T])
        sl_er = sb("sl_er", [128, 2, T])

        E, tok_lnprep = alloc_ln_env(p, nc, es, sb)
        ps_p = es.enter_context(nc.psum_tensor("ps_p", [128, 2, T], F32))

        tok_wk = p.dma("g", wkeyT[:], wkeyT_d.rearrange("(k p) f -> p k f", p=128), inc="ldw")
        tok_wv = p.dma("g", wvalT[:], wvalT_d.rearrange("(k p) f -> p k f", p=128), inc="ldw")

        xn2_free = [None]
        kk_free = [None]
        vcm_free = [None]
        a_free = [None]
        slf = {nm: [None, None] for nm in ["u", "x", "er"]}
        psp_free = [None, None]

        for st in range(NST):
            b, q = st // NQ, st % NQ

            p.wait("q", xn2_free[0])
            t_xn = p.dma("q", xn2[:], xn2_d[st].rearrange("(k p) t -> p k t", p=128), inc="ldx")

            p.wait("t", [tok_wk, tok_wv, t_xn])
            t_vcm_last = None
            for g in range(4):
                kk_toks = []
                for i in range(8):
                    fc = g * 8 + i
                    ep = fc % 2
                    p.wait("t", psp_free[ep])
                    t_mm = None
                    for dk in range(NCH):
                        t_mm = p.emit("t", lambda t, dk=dk, fc=fc, ep=ep: t.matmul(
                            ps_p[:, ep, :], wkeyT[:, dk, fc * 128:(fc + 1) * 128],
                            xn2[:, dk, :], start=(dk == 0), stop=(dk == NCH - 1)),
                            inc=("mmp" if dk == NCH - 1 else None))
                    p.wait("v", [t_mm, kk_free[0] if i == 0 else None])
                    t_rl = p.emit("v", lambda v, i=i, ep=ep: v.tensor_scalar(
                        kk[:, i, :], ps_p[:, ep, :], 0.0, None, ALU.max), inc="vk")
                    psp_free[ep] = t_rl
                    p.wait("s", t_rl)
                    t_sqk = p.emit("s", lambda s, i=i: s.activation(
                        kk[:, i, :], kk[:, i, :], AF.Square), inc="sk")
                    kk_toks.append(t_sqk)
                p.wait("t", kk_toks)
                for ec in range(NCH):
                    ep = ec % 2
                    p.wait("t", psp_free[ep])
                    t_mm = None
                    for i in range(8):
                        fc = g * 8 + i
                        t_mm = p.emit("t", lambda t, i=i, fc=fc, ec=ec, ep=ep: t.matmul(
                            ps_p[:, ep, :], wvalT[:, fc, ec * 128:(ec + 1) * 128],
                            kk[:, i, :], start=(i == 0), stop=(i == 7)),
                            inc=("mmp" if i == 7 else None))
                    p.wait("v", [t_mm, vcm_free[0] if (g == 0 and ec == 0 and st > 0) else None])
                    if g == 0:
                        t_vc = p.emit("v", lambda v, ec=ec, ep=ep: v.tensor_copy(
                            vcm[:, ec, :], ps_p[:, ep, :]), inc="vk")
                    else:
                        t_vc = p.emit("v", lambda v, ec=ec, ep=ep: v.tensor_tensor(
                            vcm[:, ec, :], vcm[:, ec, :], ps_p[:, ep, :], ALU.add), inc="vk")
                    psp_free[ep] = t_vc
                    t_vcm_last = t_vc
                kk_free[0] = t_mm

            a_toks = []
            for c in range(NCH):
                cp = c % 2
                p.wait("q", [slf["u"][cp], slf["x"][cp], slf["er"][cp]])
                t_lu = p.dma("q", sl_u[:, cp, :], u_dr[st, c * 128:(c + 1) * 128, :], inc="ldx")
                t_lx = p.dma("q", sl_x[:, cp, :],
                             x_d[b].rearrange("(k p) t -> p k t", p=128)[:, c, q * T:(q + 1) * T],
                             inc="ldx")
                t_le = p.dma("q", sl_er[:, cp, :], erec_d[st, c * 128:(c + 1) * 128, :], inc="ldx")
                p.wait("v", t_le)
                t_e1 = p.emit("v", lambda v, cp=cp: v.tensor_scalar(
                    sl_er[:, cp, :], sl_er[:, cp, :], 1.0, None, ALU.add), inc="vk")
                p.wait("s", t_e1)
                p.emit("s", lambda s, cp=cp: s.activation(
                    sl_er[:, cp, :], sl_er[:, cp, :], AF.Ln), inc=None)
                t_rc = p.emit("s", lambda s, cp=cp: s.activation(
                    sl_er[:, cp, :], sl_er[:, cp, :], AF.Exp, scale=-1.0), inc="sk")
                p.wait("v", [t_rc, t_lu, t_lx, t_vcm_last if c == 0 else None,
                             a_free[0] if c == 0 else None])
                p.emit("v", lambda v, c=c, cp=cp: v.tensor_tensor(
                    a_s[:, c, :], vcm[:, c, :], sl_er[:, cp, :], ALU.mult), inc=None)
                p.emit("v", lambda v, c=c, cp=cp: v.tensor_tensor(
                    a_s[:, c, :], a_s[:, c, :], sl_u[:, cp, :], ALU.add), inc=None)
                t_a = p.emit("v", lambda v, c=c, cp=cp: v.tensor_tensor(
                    a_s[:, c, :], a_s[:, c, :], sl_x[:, cp, :], ALU.add), inc="vk")
                slf["u"][cp] = t_a
                slf["x"][cp] = t_a
                slf["er"][cp] = t_a
                a_toks.append(t_a)
            vcm_free[0] = a_toks[-1]

            a_aps = [a_s[:, c, :] for c in range(NCH)]
            tok_rows = emit_ln_rows(p, E, a_aps, a_toks[-1])
            p.wait("v", tok_lnprep if st == 0 else None)
            t_sp = None
            x1_toks = []
            for c in range(NCH):
                t_x1 = emit_ln_apply_chunk(
                    p, E, c, a_s[:, c, :], vcm[:, c, :], tok_rows,
                    None, None, vcm_free[0] if c == 0 else None)
                x1_toks.append(t_x1)
                p.wait("q", t_x1)
                t_sp = p.dma("q", x1_d[st, c * 128:(c + 1) * 128, :], vcm[:, c, :], inc="stq")
            xn2_free[0] = t_mm
            vcm_free[0] = t_sp
            a_free[0] = [E["xsq_free"][0], E["xsq_free"][1], x1_toks[-1]]

        p.wait("q", t_sp)
        p.realize(es)
    return nc


def la3_in_maps(inp, la2_res):
    x = np.asarray(inp["L0_attn_x"], dtype=np.float32)
    wkeyT = ctr_(inp["L0_attn_Wkey"])
    wvalT = ctr_(inp["L0_attn_Wval"])
    maps = []
    for c in range(NCORE):
        sl = slice(c * SEQ, (c + 1) * SEQ)
        xc = np.ascontiguousarray(np.transpose(x[:, sl, :], (0, 2, 1)))
        r = la2_res[c]
        maps.append(dict(x=xc, u_i=r["u_o"], xn2_i=r["xn2_o"], erec_i=r["erec_o"],
                         wkeyT=wkeyT, wvalT=wvalT))
    return maps


def run_la3(inp, la2_res):
    nc = _get_nc("la3", build_la3)
    maps = la3_in_maps(inp, la2_res)
    res = _run_spmd(nc, maps)
    return res.results



def build_lb0():
    nc = bass.Bass("TRN2", target_bir_lowering=False)
    x1_d = nc.dram_tensor("x1", [NST, D, T], F32, kind="ExternalInput").ap()
    gatew_d = nc.dram_tensor("gatew", [D, 8], F32, kind="ExternalInput").ap()
    gam_d = nc.dram_tensor("gam_row", [1, D], F32, kind="ExternalInput").ap()
    bet_d = nc.dram_tensor("bet_row", [1, D], F32, kind="ExternalInput").ap()
    tbb_d = nc.dram_tensor("tbb", [128, 8], F32, kind="ExternalInput").ap()
    eye128_d = nc.dram_tensor("eye128", [128, 128], F32, kind="ExternalInput").ap()
    xn3_d = nc.dram_tensor("xn3_o", [NST, D, T], BF16, kind="ExternalOutput").ap()
    cw_d = nc.dram_tensor("cw_o", [NST, 8, T], BF16, kind="ExternalOutput").ap()

    p = Prog(nc)
    with ExitStack() as es:
        sb = lambda nm, shp, dt=F32: es.enter_context(nc.sbuf_tensor(nm, shp, dt))
        x1s = sb("x1s", [128, NCH, T])
        xn3 = sb("xn3_s", [128, NCH, T], BF16)
        gw = sb("gw_s", [128, NCH, 8], BF16)
        gam = sb("gam_s", [1, D], BF16)
        bet = sb("bet_s", [1, D], BF16)
        tbb = sb("tbb_s", [128, 8])
        eye = sb("eye_s", [128, 128])
        lsel = sb("lsel", [128, 8])
        m1 = sb("m1c", [128, 1])
        m2 = sb("m2c", [128, 1])
        f1 = sb("f1c", [128, 8])
        f2 = sb("f2c", [128, 8])
        s1c = sb("s1c", [128, 1])
        w2c = sb("w2c", [128, 1])
        cwtm = sb("cwtm", [128, 4, 8])
        cwfm = sb("cwfm", [8, T], BF16)

        E, tok_lnprep = alloc_ln_env(p, nc, es, sb)
        ps_g = es.enter_context(nc.psum_tensor("ps_g", [128, 32], F32))
        ps_tp = es.enter_context(nc.psum_tensor("ps_tp", [8, 4, 128], F32))

        tok_gw = p.dma("g", gw[:], gatew_d.rearrange("(k p) e -> p k e", p=128), inc="ldw")
        p.dma("g", gam[:], gam_d, inc="ldw")
        tok_gb = p.dma("g", bet[:], bet_d, inc="ldw")
        p.dma("q", tbb[:], tbb_d, inc="ldq")
        tok_eye = p.dma("q", eye[:], eye128_d, inc="ldq")

        x1_free = [None]
        xn3_free = [None]
        gate_free = [None]
        cw_free = [None]

        for st in range(NST):
            p.wait("q", x1_free[0])
            t_x = p.dma("q", x1s[:], x1_d[st].rearrange("(k p) t -> p k t", p=128), inc="ldx")
            x_aps = [x1s[:, c, :] for c in range(NCH)]
            tok_rows = emit_ln_rows(p, E, x_aps, t_x)
            p.wait("t", tok_gb)
            p.wait("v", tok_lnprep if st == 0 else None)
            t_xn_last = None
            xn_sp = None
            for c in range(NCH):
                t_xn_last = emit_ln_apply_chunk(
                    p, E, c, x1s[:, c, :], xn3[:, c, :], tok_rows,
                    gam, bet, xn3_free[0] if c == 0 else None)
                p.wait("q", t_xn_last)
                xn_sp = p.dma("q", xn3_d[st, c * 128:(c + 1) * 128, :], xn3[:, c, :], inc="stq")
            x1_free[0] = [E["xsq_free"][0], E["xsq_free"][1], t_xn_last]

            p.wait("t", [tok_gw, t_xn_last, gate_free[0]])
            t_g = None
            for i in range(4):
                for dk in range(NCH):
                    t_g = p.emit("t", lambda t, i=i, dk=dk: t.matmul(
                        ps_g[:, 8 * i:8 * i + 8], xn3[:, dk, i * 128:(i + 1) * 128],
                        gw[:, dk, :], start=(dk == 0), stop=(dk == NCH - 1)),
                        inc=("mmg" if dk == NCH - 1 else None))
            t_cw = None
            prev_s1 = None
            for i in range(4):
                p.wait("v", [t_g, cw_free[0]] if i == 0 else prev_s1)
                t_e = p.emit("v", lambda v, i=i: v.tensor_tensor(
                    lsel[:], ps_g[:, 8 * i:8 * i + 8], tbb[:], ALU.add), inc="vg")
                p.wait("v", t_e)
                t_e = p.emit("v", lambda v: v.reduce_max(m1[:, :], lsel[:], axis=AX.X), inc="vg")
                p.wait("v", t_e)
                t_e = p.emit("v", lambda v: v.tensor_scalar(
                    f1[:], lsel[:], m1[:, :], None, ALU.is_equal), inc="vg")
                p.wait("v", t_e)
                t_e = p.emit("v", lambda v: v.scalar_tensor_tensor(
                    lsel[:], f1[:], -1.0e30, lsel[:], ALU.mult, ALU.add), inc="vg")
                p.wait("v", t_e)
                t_e = p.emit("v", lambda v: v.reduce_max(m2[:, :], lsel[:], axis=AX.X), inc="vg")
                p.wait("v", t_e)
                t_e = p.emit("v", lambda v: v.tensor_scalar(
                    f2[:], lsel[:], m2[:, :], None, ALU.is_equal), inc="vg")
                p.wait("v", t_e)
                t_dd = p.emit("v", lambda v: v.tensor_tensor(
                    m1[:, :], m1[:, :], m2[:, :], ALU.subtract), inc="vg")
                p.wait("s", [t_dd, t_cw])
                t_s1 = p.emit("s", lambda s: s.activation(
                    s1c[:, :], m1[:, :], AF.Sigmoid), inc="sg")
                prev_s1 = t_s1
                p.wait("v", t_s1)
                t_e = p.emit("v", lambda v: v.tensor_scalar(
                    w2c[:, :], s1c[:, :], -1.0, 1.0, ALU.mult, ALU.add), inc="vg")
                p.wait("v", t_e)
                t_e = p.emit("v", lambda v, i=i: v.tensor_scalar(
                    cwtm[:, i, :], f1[:], s1c[:, :], None, ALU.mult), inc="vg")
                p.wait("v", t_e)
                t_e = p.emit("v", lambda v, i=i: v.tensor_scalar(
                    f2[:], f2[:], w2c[:, :], None, ALU.mult), inc="vg")
                p.wait("v", t_e)
                t_cw = p.emit("v", lambda v, i=i: v.tensor_tensor(
                    cwtm[:, i, :], cwtm[:, i, :], f2[:], ALU.add), inc="vg")
                p.wait("v", t_cw)
            p.wait("t", [t_cw, tok_eye])
            t_tp = None
            for i in range(4):
                t_tp = p.emit("t", lambda t, i=i: t.transpose(
                    ps_tp[:, i, :], cwtm[:, i, :], eye[:]), inc="mmg")
            p.wait("v", t_tp)
            t_ev = p.emit("v", lambda v: v.tensor_copy(
                cwfm[:].rearrange("p (i t) -> p i t", i=4), ps_tp[:, :, :]), inc="vg")
            gate_free[0] = t_ev
            cw_free[0] = t_ev
            p.wait("q", t_ev)
            t_sp = p.dma("q", cw_d[st], cwfm[:], inc="stq")

        p.wait("q", t_sp)
        p.realize(es)
    return nc



def build_lb1():
    nc = bass.Bass("TRN2", target_bir_lowering=False)
    x1_d = nc.dram_tensor("x1", [NST, D, T], F32, kind="ExternalInput").ap()
    xn3_d = nc.dram_tensor("xn3", [NST, D, T], BF16, kind="ExternalInput").ap()
    cw_d = nc.dram_tensor("cw", [NST, 8, T], BF16, kind="ExternalInput").ap()
    w1_d = nc.dram_tensor("w1", [8, D, 4096], F32, kind="ExternalInput").ap()
    w2_d = nc.dram_tensor("w2", [8, 4096, D], F32, kind="ExternalInput").ap()
    b1c_d = nc.dram_tensor("b1c", [8, 128, 32], F32, kind="ExternalInput").ap()
    b2c_d = nc.dram_tensor("b2c", [8, 128, 8], F32, kind="ExternalInput").ap()
    sel8_d = nc.dram_tensor("sel8", [8, 8, 128], F32, kind="ExternalInput").ap()
    y1_d = nc.dram_tensor("y1_o", [NST, D, T], F32, kind="ExternalOutput").ap()
    acc_d = nc.dram_tensor("acc_sc", [NST, D, T], F32, kind="Internal").ap()

    HCH = 32
    p = Prog(nc)
    with ExitStack() as es:
        sb = lambda nm, shp, dt=F32: es.enter_context(nc.sbuf_tensor(nm, shp, dt))
        w1s = sb("w1s", [128, NCH, 4096], BF16)
        w2s = sb("w2s", [128, HCH, D], BF16)
        xn3 = sb("xn3_s", [128, NCH, T], BF16)
        hbf = sb("hbf", [128, HCH, T], BF16)
        accs = sb("accs", [128, NCH, T])
        cwfm = sb("cwfm", [8, T], BF16)
        cwb = sb("cwb", [128, T])
        b1c = sb("b1c_s", [128, 32])
        b2c = sb("b2c_s", [128, 8])
        sel8 = sb("sel8_s", [8, 8, 128], BF16)
        tmp = sb("tmp_s", [128, 2, T])
        ps_p = es.enter_context(nc.psum_tensor("ps_p", [128, 2, T], F32))
        ps_cw = es.enter_context(nc.psum_tensor("ps_cw", [128, T], F32))

        tok_sel = p.dma("g", sel8[:], sel8_d.rearrange("e j p -> j e p"), inc="ldw")

        w_free = [None]
        xn_free = [None]
        acc_free = [None]
        cw_free = [None]
        tmp_free = [None, None]
        hb_free = [None]
        pscw_free = [None]
        psp_free = [None, None]

        for e in range(8):
            p.wait("g", w_free[0])
            t_w1 = p.dma("g", w1s[:], w1_d[e].rearrange("(k p) h -> p k h", p=128), inc="ldw")
            t_w2 = p.dma("g", w2s[:], w2_d[e].rearrange("(k p) h -> p k h", p=128), inc="ldw")
            t_b = p.dma("q", b1c[:], b1c_d[e], inc="ldx")
            t_b2 = p.dma("q", b2c[:], b2c_d[e], inc="ldx")

            last_mm = None
            for st in range(NST):
                p.wait("q", xn_free[0])
                t_xn = p.dma("q", xn3[:], xn3_d[st].rearrange("(k p) t -> p k t", p=128), inc="ldx")
                p.wait("q", cw_free[0])
                t_cwl = p.dma("q", cwfm[:], cw_d[st], inc="ldx")
                p.wait("q", acc_free[0])
                src_acc = x1_d if e == 0 else acc_d
                t_acc = p.dma("q", accs[:], src_acc[st].rearrange("(k p) t -> p k t", p=128), inc="ldx")

                p.wait("t", [tok_sel, t_cwl, pscw_free[0]])
                t_cb = p.emit("t", lambda t, e=e: t.matmul(
                    ps_cw[:, :], sel8[:, e, :], cwfm[:, :], start=True, stop=True), inc="mmcb")
                p.wait("v", t_cb)
                t_cbe = p.emit("v", lambda v: v.tensor_copy(cwb[:], ps_cw[:, :]), inc="vc")
                pscw_free[0] = t_cbe

                p.wait("t", [t_w1, t_xn] if st == 0 else [t_xn])
                for hk in range(HCH):
                    ep = hk % 2
                    p.wait("t", psp_free[ep])
                    t_mm = None
                    for dk in range(NCH):
                        t_mm = p.emit("t", lambda t, dk=dk, hk=hk, ep=ep: t.matmul(
                            ps_p[:, ep, :], w1s[:, dk, hk * 128:(hk + 1) * 128],
                            xn3[:, dk, :], start=(dk == 0), stop=(dk == NCH - 1)),
                            inc=("mmp" if dk == NCH - 1 else None))
                    p.wait("s", [t_mm, t_b if hk == 0 else None,
                                 hb_free[0] if hk == 0 else None])
                    t_h = p.emit("s", lambda s, hk=hk, ep=ep: s.activation(
                        hbf[:, hk, :], ps_p[:, ep, :], AF.Gelu, bias=b1c[:, hk:hk + 1]), inc="se")
                    psp_free[ep] = t_h

                p.wait("t", [t_w2, t_h] if st == 0 else [t_h])
                acc_toks = []
                for ec in range(NCH):
                    ep = ec % 2
                    p.wait("t", psp_free[ep])
                    t_mm = None
                    for hk in range(HCH):
                        t_mm = p.emit("t", lambda t, hk=hk, ec=ec, ep=ep: t.matmul(
                            ps_p[:, ep, :], w2s[:, hk, ec * 128:(ec + 1) * 128],
                            hbf[:, hk, :], start=(hk == 0), stop=(hk == HCH - 1)),
                            inc=("mmp" if hk == HCH - 1 else None))
                    last_mm = t_mm
                    cp = ec % 2
                    p.wait("v", [t_mm, t_cbe, t_acc, t_b2 if ec == 0 else None,
                                 tmp_free[cp]])
                    t_t = p.emit("v", lambda v, ec=ec, ep=ep, cp=cp: v.scalar_tensor_tensor(
                        tmp[:, cp, :], ps_p[:, ep, :], b2c[:, ec:ec + 1], cwb[:],
                        ALU.add, ALU.mult), inc="vc")
                    psp_free[ep] = t_t
                    if e == 0:
                        t_ac = p.emit("v", lambda v, ec=ec, cp=cp: v.scalar_tensor_tensor(
                            accs[:, ec, :], accs[:, ec, :], 2.0, tmp[:, cp, :],
                            ALU.mult, ALU.add), inc="vc")
                    else:
                        t_ac = p.emit("v", lambda v, ec=ec, cp=cp: v.tensor_tensor(
                            accs[:, ec, :], accs[:, ec, :], tmp[:, cp, :], ALU.add), inc="vc")
                    tmp_free[cp] = t_ac
                    acc_toks.append(t_ac)
                    p.wait("q", t_ac)
                    dst = y1_d if e == 7 else acc_d
                    t_sp = p.dma("q", dst[st, ec * 128:(ec + 1) * 128, :],
                                 accs[:, ec, :], inc="stq")
                xn_free[0] = t_mm
                cw_free[0] = t_cb
                acc_free[0] = [acc_toks[-1], t_sp]
                hb_free[0] = last_mm
            w_free[0] = last_mm

        p.wait("q", t_sp)
        p.realize(es)
    return nc


def lb_in_maps(inp, la3_res, lb0_res=None):
    gatew = np.ascontiguousarray(np.asarray(inp["L0_ffn_gate_w"], dtype=np.float32))
    gam = row_(inp["L0_ffn_gamma"])
    bet = row_(inp["L0_ffn_beta"])
    tbb = np.tile((np.arange(8)[::-1].astype(np.float32) * 1e-6).reshape(1, 8), (128, 1))
    eye128 = np.eye(128, dtype=np.float32)
    w1 = np.ascontiguousarray(np.asarray(inp["L0_ffn_w1"], dtype=np.float32))
    w2 = np.ascontiguousarray(np.asarray(inp["L0_ffn_w2"], dtype=np.float32))
    b1c = np.stack([np.asarray(inp["L0_ffn_b1"], np.float32)[e].reshape(32, 128).T for e in range(8)])
    b2c = np.stack([np.asarray(inp["L0_ffn_b2"], np.float32)[e].reshape(8, 128).T for e in range(8)])
    sel8 = np.zeros((8, 8, 128), np.float32)
    for e in range(8):
        sel8[e, e, :] = 1.0
    maps0, maps1 = [], []
    for c in range(NCORE):
        x1 = la3_res[c]["x1_o"]
        maps0.append(dict(x1=x1, gatew=gatew, gam_row=gam, bet_row=bet,
                          tbb=np.ascontiguousarray(tbb), eye128=eye128))
        if lb0_res is not None:
            r0 = lb0_res[c]
            maps1.append(dict(x1=x1, xn3=r0["xn3_o"], cw=r0["cw_o"], w1=w1, w2=w2,
                              b1c=b1c, b2c=b2c, sel8=sel8))
    return maps0, maps1


def run_lb(inp, la3_res):
    nc0 = _get_nc("lb0", build_lb0)
    maps0, _ = lb_in_maps(inp, la3_res)
    res0 = _run_spmd(nc0, maps0).results
    nc1 = _get_nc("lb1", build_lb1)
    _, maps1 = lb_in_maps(inp, la3_res, res0)
    res1 = _run_spmd(nc1, maps1).results
    return res0, res1



def build_lc1():
    nc = bass.Bass("TRN2", target_bir_lowering=False)
    y1_d = nc.dram_tensor("y1", [NST, D, T], F32, kind="ExternalInput").ap()
    w6_d = nc.dram_tensor("w6", [6, D, D], F32, kind="ExternalInput").ap()
    mu_d = nc.dram_tensor("mu6", [6, 128, NCH], F32, kind="ExternalInput").ap()
    w0_d = nc.dram_tensor("w0c", [128, NCH], F32, kind="ExternalInput").ap()
    a0_d = nc.dram_tensor("a0c", [128, NCH], F32, kind="ExternalInput").ap()
    x2_d = nc.dram_tensor("x2_o", [NQ, 2, D, T], F32, kind="ExternalOutput").ap()
    x2b_d = nc.dram_tensor("x2b_o", [NQ, 2, D, T], BF16, kind="ExternalOutput").ap()
    r_d = nc.dram_tensor("r_o", [NQ, 2, D, T], BF16, kind="ExternalOutput").ap()
    w1_d = nc.dram_tensor("w1_o", [NQ, D, T], BF16, kind="ExternalOutput").ap()
    k_d = nc.dram_tensor("k_o", [NQ, 2, D, T], F32, kind="ExternalOutput").ap()
    v_d = nc.dram_tensor("v_o", [NQ, 2, D, T], F32, kind="ExternalOutput").ap()
    a_d = nc.dram_tensor("a_o", [NQ, 2, D, T], F32, kind="ExternalOutput").ap()
    g_d = nc.dram_tensor("g_o", [NQ, 2, D, T], F32, kind="ExternalOutput").ap()

    ECN = np.exp(-0.5)
    p = Prog(nc)
    with ExitStack() as es:
        sb = lambda nm, shp, dt=F32: es.enter_context(nc.sbuf_tensor(nm, shp, dt))
        wraw = sb("wraw", [128, NCH, D], BF16)
        wp = sb("wp", [128, NCH, D], BF16)
        wpp = sb("wpp", [128, NCH, D], BF16)
        y1s = sb("y1s", [128, NCH, T])
        xbf = sb("xbf_s", [128, 2, NCH, T], BF16)
        mu6 = sb("mu6_s", [128, 6, NCH])
        omu = sb("omu_s", [128, NCH])
        w0c = sb("w0c_s", [128, NCH])
        a0c = sb("a0c_s", [128, NCH])
        nw0 = sb("nw0_s", [128, NCH])
        na0 = sb("na0_s", [128, NCH])
        sl = sb("sl_e", [128, 2, T])
        sl2 = sb("sl2_e", [128, 2, T])
        slb = sb("slb_e", [128, 2, T], BF16)
        E, tok_lnprep = alloc_ln_env(p, nc, es, sb)
        ps_p = es.enter_context(nc.psum_tensor("ps_p", [128, 2, T], F32))

        tok_mu = p.dma("q", mu6[:], mu_d.rearrange("m p c -> p m c"), inc="ldq")
        p.dma("q", w0c[:], w0_d, inc="ldq")
        tok_b0 = p.dma("q", a0c[:], a0_d, inc="ldq")
        p.wait("v", tok_b0)
        p.emit("v", lambda v: v.tensor_scalar(nw0[:], w0c[:], -1.0, None, ALU.mult), inc=None)
        t_neg = p.emit("v", lambda v: v.tensor_scalar(na0[:], a0c[:], -1.0, None, ALU.mult), inc="vr")

        y1_free = [None]
        xb_free = [None]
        for q in range(NQ):
            for t in range(2):
                st = t * NQ + q
                p.wait("q", y1_free[0])
                t_y = p.dma("q", y1s[:], y1_d[st].rearrange("(k p) t -> p k t", p=128), inc="ldx")
                y_aps = [y1s[:, c, :] for c in range(NCH)]
                tok_rows = emit_ln_rows(p, E, y_aps, t_y)
                p.wait("v", tok_lnprep if st == 0 else None)
                t_xn = None
                for c in range(NCH):
                    t_xn = emit_ln_apply_chunk(
                        p, E, c, y1s[:, c, :], y1s[:, c, :],
                        tok_rows, None, None, None)
                    p.wait("q", t_xn)
                    p.dma("q", x2_d[q, t, c * 128:(c + 1) * 128, :], y1s[:, c, :], inc="stq")
                    p.wait("v", xb_free[0] if c == 0 else None)
                    t_xb = p.emit("v", lambda v, c=c, t=t: v.tensor_copy(
                        xbf[:, t, c, :], y1s[:, c, :]), inc="vr")
                    xb_free[0] = None
                    p.wait("q", t_xb)
                    t_spb = p.dma("q", x2b_d[q, t, c * 128:(c + 1) * 128, :], xbf[:, t, c, :], inc="stq")
                y1_free[0] = [t_xb, t_spb]
                xb_free[0] = t_spb
        xb_free[0] = None

        PROJ = ["r", "w", "k", "v", "a", "g"]
        wfree = [None]
        slf = [None, None]
        sl2f = [None, None]
        slbf = [None, None]
        psf = [None, None]
        prev_sc = [None]
        for m, nm in enumerate(PROJ):
            p.wait("g", prev_sc[0])
            t_wl = p.dma("g", wraw[:], w6_d[m].rearrange("(k p) e -> p k e", p=128), inc="ldw")
            p.wait("v", [t_wl, tok_mu, wfree[0]])
            t_sc = None
            for dk in range(NCH):
                p.emit("v", lambda v, dk=dk, m=m: v.tensor_scalar(
                    wpp[:, dk, :], wraw[:, dk, :], mu6[:, m, dk:dk + 1], None, ALU.mult), inc=None)
                t_sc = p.emit("v", lambda v, dk=dk: v.tensor_tensor(
                    wp[:, dk, :], wraw[:, dk, :], wpp[:, dk, :], ALU.subtract), inc="vr")
            p.wait("t", t_sc)

            for q in range(NQ):
                for t in range(2):
                    if nm == "w" and t == 0:
                        continue
                    p.wait("q", xb_free[0])
                    t_x1 = p.dma("q", xbf[:, 1, :, :],
                                 x2b_d[q, t].rearrange("(k p) u -> p k u", p=128), inc="ldx")
                    t_x0 = t_x1
                    if t == 1:
                        t_x0 = p.dma("q", xbf[:, 0, :, :],
                                     x2b_d[q, 0].rearrange("(k p) u -> p k u", p=128), inc="ldx")
                    p.wait("t", [t_x1, t_x0])
                    for ec in range(NCH):
                        ep = ec % 2
                        p.wait("t", psf[ep])
                        t_mm = None
                        for dk in range(NCH):
                            t_mm = p.emit("t", lambda tt, dk=dk, ec=ec, ep=ep, t=t: tt.matmul(
                                ps_p[:, ep, :], wp[:, dk, ec * 128:(ec + 1) * 128],
                                xbf[:, 1, dk, :], start=(dk == 0), stop=(dk == NCH - 1 and t == 0)),
                                inc=("mmp" if (dk == NCH - 1 and t == 0) else None))
                        if t == 1:
                            for dk in range(NCH):
                                t_mm = p.emit("t", lambda tt, dk=dk, ec=ec, ep=ep: tt.matmul(
                                    ps_p[:, ep, :], wpp[:, dk, ec * 128:(ec + 1) * 128],
                                    xbf[:, 0, dk, :], start=False, stop=(dk == NCH - 1)),
                                    inc=("mmp" if dk == NCH - 1 else None))
                        if nm == "r":
                            p.wait("s", [t_mm, slbf[ep]])
                            t_ev = p.emit("s", lambda s, ep=ep: s.activation(
                                slb[:, ep, :], ps_p[:, ep, :], AF.Copy), inc="se")
                            psf[ep] = t_ev
                            p.wait("q", t_ev)
                            t_sp = p.dma("q", r_d[q, t, ec * 128:(ec + 1) * 128, :],
                                         slb[:, ep, :], inc="stq")
                            slbf[ep] = t_sp
                        elif nm in ("k", "v"):
                            dst = k_d if nm == "k" else v_d
                            p.wait("s", [t_mm, slf[ep]])
                            t_ev = p.emit("s", lambda s, ep=ep: s.copy(
                                sl[:, ep, :], ps_p[:, ep, :]), inc="se")
                            psf[ep] = t_ev
                            p.wait("q", t_ev)
                            t_sp = p.dma("q", dst[q, t, ec * 128:(ec + 1) * 128, :],
                                         sl[:, ep, :], inc="stq")
                            slf[ep] = t_sp
                        else:
                            bias = {"w": nw0, "a": na0, "g": None}[nm]
                            p.wait("s", [t_mm, slf[ep], t_neg])
                            if bias is None:
                                t_e1 = p.emit("s", lambda s, ep=ep: s.activation(
                                    sl[:, ep, :], ps_p[:, ep, :], AF.Exp, scale=-1.0), inc="se")
                            else:
                                t_e1 = p.emit("s", lambda s, ep=ep, bias=bias, ec=ec: s.activation(
                                    sl[:, ep, :], ps_p[:, ep, :], AF.Exp, scale=-1.0,
                                    bias=bias[:, ec:ec + 1]), inc="se")
                            psf[ep] = t_e1
                            p.wait("v", [t_e1, sl2f[ep]])
                            t_p1 = p.emit("v", lambda v, ep=ep: v.tensor_scalar(
                                sl2[:, ep, :], sl[:, ep, :], 1.0, None, ALU.add), inc="vr")
                            p.wait("s", t_p1)
                            p.emit("s", lambda s, ep=ep: s.activation(
                                sl2[:, ep, :], sl2[:, ep, :], AF.Ln), inc=None)
                            if nm == "w":
                                p.emit("s", lambda s, ep=ep: s.activation(
                                    sl2[:, ep, :], sl2[:, ep, :], AF.Exp, scale=-1.0), inc=None)
                                p.wait("s", slbf[ep])
                                t_sg = p.emit("s", lambda s, ep=ep: s.activation(
                                    slb[:, ep, :], sl2[:, ep, :], AF.Exp, scale=-ECN), inc="se")
                                p.wait("q", t_sg)
                                t_sp = p.dma("q", w1_d[q, ec * 128:(ec + 1) * 128, :],
                                             slb[:, ep, :], inc="stq")
                                slbf[ep] = t_sp
                                sl2f[ep] = t_sg
                            else:
                                t_sg = p.emit("s", lambda s, ep=ep: s.activation(
                                    sl2[:, ep, :], sl2[:, ep, :], AF.Exp, scale=-1.0), inc="se")
                                p.wait("q", t_sg)
                                dst = a_d if nm == "a" else g_d
                                t_sp = p.dma("q", dst[q, t, ec * 128:(ec + 1) * 128, :],
                                             sl2[:, ep, :], inc="stq")
                                sl2f[ep] = t_sp
                    xb_free[0] = t_mm
            wfree[0] = t_mm
            prev_sc[0] = t_sc

        p.wait("q", t_sp)
        p.realize(es)
    return nc



def build_lc2():
    nc = bass.Bass("TRN2", target_bir_lowering=False)
    x2_d = nc.dram_tensor("x2", [NQ, 2, D, T], F32, kind="ExternalInput").ap()
    r_d = nc.dram_tensor("r_i", [NQ, 2, D, T], BF16, kind="ExternalInput").ap()
    w1_d = nc.dram_tensor("w1_i", [NQ, D, T], BF16, kind="ExternalInput").ap()
    k_d = nc.dram_tensor("k_i", [NQ, 2, D, T], F32, kind="ExternalInput").ap()
    v_d = nc.dram_tensor("v_i", [NQ, 2, D, T], F32, kind="ExternalInput").ap()
    a_d = nc.dram_tensor("a_i", [NQ, 2, D, T], F32, kind="ExternalInput").ap()
    g_d = nc.dram_tensor("g_i", [NQ, 2, D, T], F32, kind="ExternalInput").ap()
    wo_d = nc.dram_tensor("wo", [D, D], F32, kind="ExternalInput").ap()
    wf_d = nc.dram_tensor("wf", [D, D], F32, kind="ExternalInput").ap()
    hpat_d = nc.dram_tensor("hpat", [8, 128, 16], F32, kind="ExternalInput").ap()
    hpatT_d = nc.dram_tensor("hpatT", [8, 16, 128], F32, kind="ExternalInput").ap()
    COLN = ["kkc", "kac", "rkc", "lnxw", "lnxb", "nrmw", "nrmb", "bfc", "lnfw", "lnfb"]
    cold = {nm: nc.dram_tensor(nm, [128, NCH], F32, kind="ExternalInput").ap() for nm in COLN}
    x3_d = nc.dram_tensor("x3_o", [NST, D, T], F32, kind="ExternalOutput").ap()
    dbgy_d = nc.dram_tensor("dbgy_o", [2, D, T], F32, kind="ExternalOutput").ap()
    dbgp_d = nc.dram_tensor("dbgp_o", [16, 10, T], BF16, kind="ExternalOutput").ap()
    dbgkp_d = nc.dram_tensor("dbgkp_o", [2, D, T], BF16, kind="ExternalOutput").ap()
    dbgkk_d = nc.dram_tensor("dbgkk_o", [D, T], BF16, kind="ExternalOutput").ap()

    p = Prog(nc)
    with ExitStack() as es:
        sb = lambda nm, shp, dt=F32: es.enter_context(nc.sbuf_tensor(nm, shp, dt))
        wo = sb("wo_s", [128, NCH, D], BF16)
        wf = sb("wf_s", [128, NCH, D], BF16)
        slx = sb("slx_s", [128, 2, T])
        kp = sb("kp_s", [128, 2, NCH, T], BF16)
        kk1 = sb("kk1_s", [128, NCH, T], BF16)
        r1 = sb("r1_s", [128, NCH, T], BF16)
        yv = sb("yv_s", [128, NCH, T])
        us = sb("us_s", [128, NCH, T])
        fs = sb("fs_s", [128, NCH, T])
        zbf = sb("zbf_s", [128, NCH, T], BF16)
        hbf = zbf
        hpat = sb("hpat_s", [128, NCH, 16], BF16)
        hpatT = sb("hpatT_s", [16, NCH, 128], BF16)
        Psb = sb("Psb_s", [16, 10, T], BF16)
        Prow = sb("Prow_s", [16, 4, T])
        gnep = sb("gnep_s", [16, 1])
        cols = {nm: sb(nm + "_s", [128, NCH]) for nm in COLN}
        onemka = sb("onemka_s", [128, NCH])
        nbfc = sb("nbfc_s", [128, NCH])
        slot = {nm: sb("lc2_" + nm, [128, 2, T]) for nm in ["k0", "k1", "a0", "a1"]}
        slot["vv0"] = slot["k0"]; slot["vv1"] = slot["k1"]; slot["gg"] = slot["a1"]
        slotb = {nm: sb("lc2b_" + nm, [128, 2, T], BF16) for nm in ["r0", "w1l", "pr", "pr2", "ysq"]}

        p.bscr = sb("bscr", [1, 4])
        E, tok_lnprep = alloc_ln_env(p, nc, es, sb)
        ps_p = es.enter_context(nc.psum_tensor("ps_p", [128, 2, T], F32))
        PSLOT = {"ab0": (E["ps_ab"], 0), "ab1": (E["ps_ab"], 1),
                 "bb0": (E["ps_bb"], 0), "bb1": (E["ps_bb"], 1),
                 "p0": (ps_p, 0), "p1": (ps_p, 1)}

        def red_ap(s_):
            tns, idx = PSLOT[s_]
            return tns[0:16, idx, :]

        def bc_ap(s_):
            tns, idx = PSLOT[s_]
            return tns[:, idx, :]

        t_w = p.dma("g", wo[:], wo_d.rearrange("(k p) e -> p k e", p=128), inc="ldw")
        t_w = p.dma("g", wf[:], wf_d.rearrange("(k p) e -> p k e", p=128), inc="ldw")
        p.dma("g", hpat[:], hpat_d.rearrange("c p j -> p c j"), inc="ldw")
        t_hp = p.dma("g", hpatT[:], hpatT_d.rearrange("c j p -> j c p"), inc="ldw")
        tok_w, tok_hp = t_w, t_hp
        t_cl = None
        for nm in COLN:
            t_cl = p.dma("q", cols[nm][:], cold[nm], inc="ldq")
        p.wait("v", t_cl)
        p.emit("v", lambda v: v.tensor_scalar(onemka[:], cols["kac"][:], -1.0, 1.0, ALU.mult, ALU.add), inc=None)
        p.emit("v", lambda v: v.tensor_scalar(nbfc[:], cols["bfc"][:], -1.0, None, ALU.mult), inc=None)
        t_prep = p.emit("v", lambda v: v.memset(gnep[:], 0.00064), inc="vr")
        p.wait("t", [tok_hp])
        p.wait("v", tok_lnprep)
        p.barrier()

        PIDX = {"P0": 1, "P1": 2, "P4": 3, "P50": 4, "P51": 5, "P2": 6, "P3": 7, "Q": 8}
        slxf = [None, None]
        t_sp = None
        for q in range(NQ):
            bt, bv, bs = p.barrier()
            p.wait("q", [bt, bv, bs])
            t_r1l = p.dma("q", r1[:], r_d[q, 1].rearrange("(k p) u -> p k u", p=128), inc="ldx")

            RED1 = {"ss": "p0", "P0": "p1", "P1": "ab0", "P4": "ab1", "P50": "bb0", "P51": "bb1"}
            t_red = {}
            slf = {nm: [None, None] for nm in slot}
            slbf = {nm: [None, None] for nm in slotb}
            for c in range(NCH):
                cp = c % 2
                p.wait("q", [slf["k0"][cp], slf["k1"][cp], slf["a0"][cp], slf["a1"][cp],
                             slbf["r0"][cp], slbf["w1l"][cp], slbf["pr"][cp], slbf["pr2"][cp]])
                t_k0 = p.dma("q", slot["k0"][:, cp, :], k_d[q, 0, c * 128:(c + 1) * 128, :], inc="ldx")
                t_k1 = p.dma("q", slot["k1"][:, cp, :], k_d[q, 1, c * 128:(c + 1) * 128, :], inc="ldx")
                t_a0 = p.dma("q", slot["a0"][:, cp, :], a_d[q, 0, c * 128:(c + 1) * 128, :], inc="ldx")
                t_a1 = p.dma("q", slot["a1"][:, cp, :], a_d[q, 1, c * 128:(c + 1) * 128, :], inc="ldx")
                t_r0 = p.dma("q", slotb["r0"][:, cp, :], r_d[q, 0, c * 128:(c + 1) * 128, :], inc="ldx")
                t_w1 = p.dma("q", slotb["w1l"][:, cp, :], w1_d[q, c * 128:(c + 1) * 128, :], inc="ldx")
                p.wait("v", [t_k0, t_a0])
                p.emit("v", lambda v, c=c, cp=cp: v.tensor_scalar(
                    slot["a0"][:, cp, :], slot["a0"][:, cp, :], cols["kac"][:, c:c + 1],
                    onemka[:, c:c + 1], ALU.mult, ALU.add), inc=None)
                t_kp0 = p.emit("v", lambda v, c=c, cp=cp: v.tensor_tensor(
                    kp[:, 0, c, :], slot["k0"][:, cp, :], slot["a0"][:, cp, :], ALU.mult), inc="vr")
                p.wait("v", [t_k1, t_a1])
                p.emit("v", lambda v, c=c, cp=cp: v.tensor_scalar(
                    slot["a1"][:, cp, :], slot["a1"][:, cp, :], cols["kac"][:, c:c + 1],
                    onemka[:, c:c + 1], ALU.mult, ALU.add), inc=None)
                t_kp1 = p.emit("v", lambda v, c=c, cp=cp: v.tensor_tensor(
                    kp[:, 1, c, :], slot["k1"][:, cp, :], slot["a1"][:, cp, :], ALU.mult), inc="vr")
                t_kk1 = p.emit("v", lambda v, c=c, cp=cp: v.tensor_scalar(
                    kk1[:, c, :], slot["k1"][:, cp, :], cols["kkc"][:, c:c + 1], None, ALU.mult), inc="vr")

                def red(s_, rhs_ap, c2, tag):
                    tok = p.emit("t", lambda tt, rhs_ap=rhs_ap, s_=s_, c2=c2: tt.matmul(
                        red_ap(s_), hpat[:, c2, :], rhs_ap,
                        start=(c2 == 0), stop=(c2 == NCH - 1)), inc="mmr")
                    return tok

                p.wait("v", [t_r0, t_r1l if c == 0 else None, slbf["pr"][cp]])
                t_pr0 = p.emit("v", lambda v, c=c, cp=cp: v.tensor_tensor(
                    slotb["pr"][:, cp, :], kp[:, 0, c, :], slotb["r0"][:, cp, :], ALU.mult), inc="vr")
                p.wait("t", t_pr0)
                t_red["P0"] = red(RED1["P0"], slotb["pr"][:, cp, :], c, "P0")
                p.wait("v", slbf["pr2"][cp])
                t_p50 = p.emit("v", lambda v, c=c, cp=cp: v.tensor_scalar(
                    slotb["pr2"][:, cp, :], slotb["pr"][:, cp, :], cols["rkc"][:, c:c + 1],
                    None, ALU.mult), inc="vr")
                p.wait("t", t_p50)
                t_red["P50"] = red(RED1["P50"], slotb["pr2"][:, cp, :], c, "P50")
                p.wait("v", [t_w1, t_red["P0"]])
                p.emit("v", lambda v, c=c, cp=cp: v.tensor_tensor(
                    slotb["pr"][:, cp, :], kp[:, 0, c, :], slotb["w1l"][:, cp, :], ALU.mult), inc=None)
                t_pr1 = p.emit("v", lambda v, c=c, cp=cp: v.tensor_tensor(
                    slotb["pr"][:, cp, :], slotb["pr"][:, cp, :], r1[:, c, :], ALU.mult), inc="vr")
                p.wait("t", t_pr1)
                t_red["P1"] = red(RED1["P1"], slotb["pr"][:, cp, :], c, "P1")
                p.wait("v", t_red["P1"])
                t_pr4 = p.emit("v", lambda v, c=c, cp=cp: v.tensor_tensor(
                    slotb["pr"][:, cp, :], kp[:, 1, c, :], r1[:, c, :], ALU.mult), inc="vr")
                p.wait("t", t_pr4)
                t_red["P4"] = red(RED1["P4"], slotb["pr"][:, cp, :], c, "P4")
                p.wait("v", [t_red["P50"], slbf["pr2"][cp]])
                t_p51 = p.emit("v", lambda v, c=c, cp=cp: v.tensor_scalar(
                    slotb["pr2"][:, cp, :], slotb["pr"][:, cp, :], cols["rkc"][:, c:c + 1],
                    None, ALU.mult), inc="vr")
                p.wait("t", t_p51)
                t_red["P51"] = red(RED1["P51"], slotb["pr2"][:, cp, :], c, "P51")
                p.wait("s", [t_kk1, slbf["ysq"][cp]])
                t_sq = p.emit("s", lambda s, c=c, cp=cp: s.activation(
                    slotb["ysq"][:, cp, :], kk1[:, c, :], AF.Square), inc="se")
                p.wait("t", t_sq)
                t_red["ss"] = red(RED1["ss"], slotb["ysq"][:, cp, :], c, "ss")
                slf["k0"][cp] = t_kp0
                slf["k1"][cp] = t_kk1
                slf["a0"][cp] = t_kp0
                slf["a1"][cp] = t_kp1
                slbf["r0"][cp] = t_red["P0"]
                slbf["w1l"][cp] = t_pr1
                slbf["pr"][cp] = t_red["P4"]
                slbf["pr2"][cp] = t_red["P51"]
                slbf["ysq"][cp] = t_red["ss"]

            if q == 0:
                p.wait("q", t_kp1)
                for t2 in range(2):
                    p.dma("q", dbgkp_d[t2].rearrange("(k p) u -> p k u", p=128),
                          kp[:, t2, :, :], inc="stq")
            p.wait("v", [t_red[k] for k in RED1])
            for k in ["P0", "P1", "P4", "P50", "P51"]:
                p.emit("v", lambda v, k=k: v.tensor_copy(Psb[:, PIDX[k], :], red_ap(RED1[k])), inc=None)
            t_ssv = p.emit("v", lambda v: v.tensor_scalar(
                Prow[:, 0, :], red_ap(RED1["ss"]), 1e-24, None, ALU.max), inc="vr")
            p.wait("s", t_ssv)
            p.emit("s", lambda s: s.activation(Prow[:, 0, :], Prow[:, 0, :], AF.Ln), inc=None)
            t_rsk = p.emit("s", lambda s: s.activation(
                Psb[:, 0, :], Prow[:, 0, :], AF.Exp, scale=-0.5), inc="se")
            p.barrier()

            slbf2 = {nm: [None, None] for nm in slotb}
            slf2 = {nm: [None, None] for nm in slot}
            t_red2 = {}
            prev_pr2 = None
            for c in range(NCH):
                cp = c % 2
                p.wait("t", prev_pr2)
                t_bc = p.emit("t", lambda tt, c=c: tt.matmul(
                    bc_ap("p0"), hpatT[:, c, :], Psb[:, 0, :], start=True, stop=True), inc="mmr")
                p.wait("v", [t_bc, slbf2["pr"][cp]])
                p.emit("v", lambda v, c=c: v.tensor_tensor(
                    kk1[:, c, :], kk1[:, c, :], bc_ap("p0"), ALU.mult), inc=None)
                t_pr2 = p.emit("v", lambda v, c=c, cp=cp: v.tensor_tensor(
                    slotb["pr"][:, cp, :], kp[:, 0, c, :], kk1[:, c, :], ALU.mult), inc="vr")
                p.wait("t", t_pr2)
                t_red2["P2"] = p.emit("t", lambda tt, c=c, cp=cp: tt.matmul(
                    red_ap("p1"), hpat[:, c, :], slotb["pr"][:, cp, :],
                    start=(c == 0), stop=(c == NCH - 1)), inc="mmr")
                p.wait("q", slf2["a1"][cp])
                t_a1r = p.dma("q", slot["a1"][:, cp, :], a_d[q, 1, c * 128:(c + 1) * 128, :], inc="ldx")
                p.wait("v", [t_a1r, slbf2["pr2"][cp]])
                p.emit("v", lambda v, c=c, cp=cp: v.tensor_tensor(
                    slotb["pr2"][:, cp, :], kk1[:, c, :], slot["a1"][:, cp, :], ALU.mult), inc=None)
                t_pr3 = p.emit("v", lambda v, c=c, cp=cp: v.tensor_tensor(
                    slotb["pr2"][:, cp, :], slotb["pr2"][:, cp, :], r1[:, c, :], ALU.mult), inc="vr")
                p.wait("t", t_pr3)
                t_red2["P3"] = p.emit("t", lambda tt, c=c, cp=cp: tt.matmul(
                    red_ap("ab0"), hpat[:, c, :], slotb["pr2"][:, cp, :],
                    start=(c == 0), stop=(c == NCH - 1)), inc="mmr")
                slf2["a1"][cp] = t_pr3
                slbf2["pr"][cp] = t_red2["P2"]
                slbf2["pr2"][cp] = t_red2["P3"]
                prev_pr2 = t_pr2
            if q == 0:
                p.wait("q", t_pr3)
                p.dma("q", dbgkk_d.rearrange("(k p) u -> p k u", p=128), kk1[:], inc="stq")
            p.wait("v", [t_red2["P2"], t_red2["P3"]])
            p.emit("v", lambda v: v.tensor_copy(Psb[:, PIDX["P2"], :], red_ap("p1")), inc=None)
            p.emit("v", lambda v: v.tensor_copy(Psb[:, PIDX["P3"], :], red_ap("ab0")), inc=None)
            p.emit("v", lambda v: v.tensor_tensor(
                Prow[:, 1, :], Psb[:, PIDX["P2"], :], Psb[:, PIDX["P3"], :], ALU.mult), inc=None)
            t_Q = p.emit("v", lambda v: v.tensor_tensor(
                Psb[:, PIDX["Q"], :], Psb[:, PIDX["P1"], :], Prow[:, 1, :], ALU.subtract), inc="vr")
            if q == 0:
                p.wait("q", t_Q)
                p.dma("q", dbgp_d, Psb[:], inc="stq")
            p.barrier()

            for t in range(2):
                slf3 = {nm: [None, None] for nm in slot}
                t_y_last = None
                for c in range(NCH):
                    cp = c % 2
                    p.wait("q", [slf3["vv0"][cp], slf3["vv1"][cp]])
                    t_v0 = p.dma("q", slot["vv0"][:, cp, :], v_d[q, 0, c * 128:(c + 1) * 128, :], inc="ldx")
                    t_v1 = t_v0
                    if t == 1:
                        t_v1 = p.dma("q", slot["vv1"][:, cp, :], v_d[q, 1, c * 128:(c + 1) * 128, :], inc="ldx")
                    if t == 0:
                        t_b1 = p.emit("t", lambda tt, c=c: tt.matmul(
                            bc_ap("p0"), hpatT[:, c, :], Psb[:, PIDX["P0"], :],
                            start=True, stop=True), inc="mmr")
                        p.wait("v", [t_b1, t_v0])
                        t_y = p.emit("v", lambda v, c=c, cp=cp, t=t: v.tensor_tensor(
                            yv[:, c, :], slot["vv0"][:, cp, :], bc_ap("p0"), ALU.mult), inc="vr")
                        p.wait("t", t_y)
                    else:
                        t_b1 = p.emit("t", lambda tt, c=c: tt.matmul(
                            bc_ap("p0"), hpatT[:, c, :], Psb[:, PIDX["Q"], :],
                            start=True, stop=True), inc="mmr")
                        t_b2 = p.emit("t", lambda tt, c=c: tt.matmul(
                            bc_ap("p1"), hpatT[:, c, :], Psb[:, PIDX["P4"], :],
                            start=True, stop=True), inc="mmr")
                        p.wait("v", [t_b1, t_b2, t_v0, t_v1])
                        p.emit("v", lambda v, c=c, cp=cp: v.tensor_tensor(
                            yv[:, c, :], slot["vv0"][:, cp, :], bc_ap("p0"), ALU.mult), inc=None)
                        p.emit("v", lambda v, c=c, cp=cp: v.tensor_tensor(
                            slot["vv1"][:, cp, :], slot["vv1"][:, cp, :], bc_ap("p1"), ALU.mult), inc=None)
                        t_y = p.emit("v", lambda v, c=c, cp=cp: v.tensor_tensor(
                            yv[:, c, :], yv[:, c, :], slot["vv1"][:, cp, :], ALU.add), inc="vr")
                        p.wait("t", t_y)
                    slf3["vv0"][cp] = t_y
                    slf3["vv1"][cp] = t_y
                    t_y_last = t_y
                if q == 0:
                    p.wait("q", t_y_last)
                    p.dma("q", dbgy_d[t].rearrange("(k p) u -> p k u", p=128), yv[:], inc="stq")
                p.barrier()

                slb3 = [None, None]
                t_gnA = t_gnB = None
                for c in range(NCH):
                    cp = c % 2
                    p.wait("s", slb3[cp])
                    t_yb = p.emit("s", lambda s, c=c, cp=cp, t=t: s.activation(
                        slotb["ysq"][:, cp, :], yv[:, c, :], AF.Copy), inc="se")
                    p.wait("t", t_yb)
                    t_gnA = p.emit("t", lambda tt, c=c, cp=cp: tt.matmul(
                        red_ap("bb0"), hpat[:, c, :], slotb["ysq"][:, cp, :],
                        start=(c == 0), stop=(c == NCH - 1)), inc="mmr")
                    t_ysq = p.emit("s", lambda s, c=c, cp=cp, t=t: s.activation(
                        slotb["pr"][:, cp, :], yv[:, c, :], AF.Square), inc="se")
                    p.wait("t", t_ysq)
                    t_gnB = p.emit("t", lambda tt, c=c, cp=cp: tt.matmul(
                        red_ap("bb1"), hpat[:, c, :], slotb["pr"][:, cp, :],
                        start=(c == 0), stop=(c == NCH - 1)), inc="mmr")
                    slb3[cp] = [t_gnA, t_gnB]
                p.wait("v", [t_gnA, t_gnB, t_prep])
                t_mu = p.emit("v", lambda v: v.tensor_scalar(
                    Prow[:, 0, :], red_ap("bb0"), 1.0 / 64, None, ALU.mult), inc="vr")
                p.wait("v", t_mu)
                p.emit("v", lambda v: v.scalar_tensor_tensor(
                    Prow[:, 1, :], Prow[:, 0, :], -1.0, Prow[:, 0, :], ALU.mult, ALU.mult), inc=None)
                t_var = p.emit("v", lambda v: v.scalar_tensor_tensor(
                    Prow[:, 1, :], red_ap("bb1"), 1.0 / 64, Prow[:, 1, :], ALU.mult, ALU.add), inc="vr")
                p.wait("s", t_var)
                p.emit("s", lambda s: s.activation(
                    Prow[:, 1, :], Prow[:, 1, :], AF.Ln, bias=gnep[:, :]), inc=None)
                t_rsy = p.emit("s", lambda s: s.activation(
                    Prow[:, 2, :], Prow[:, 1, :], AF.Exp, scale=-0.5), inc="se")
                p.wait("v", t_rsy)
                p.emit("v", lambda v: v.tensor_copy(Psb[:, 0, :], Prow[:, 0, :]), inc=None)
                t_bcp = p.emit("v", lambda v: v.tensor_copy(Psb[:, 9, :], Prow[:, 2, :]), inc="vr")
                p.barrier()

                slf4 = {nm: [None, None] for nm in ["gg", "vvt"]}
                vslot = "a0"
                t_z = None
                for c in range(NCH):
                    cp = c % 2
                    t_bmu = p.emit("t", lambda tt, c=c: tt.matmul(
                        bc_ap("p0"), hpatT[:, c, :], Psb[:, 0, :], start=True, stop=True), inc="mmr")
                    t_brs = p.emit("t", lambda tt, c=c: tt.matmul(
                        bc_ap("p1"), hpatT[:, c, :], Psb[:, 9, :], start=True, stop=True), inc="mmr")
                    t_b5 = p.emit("t", lambda tt, c=c, t=t: tt.matmul(
                        bc_ap("ab0"), hpatT[:, c, :], Psb[:, PIDX["P50"] if t == 0 else PIDX["P51"], :],
                        start=True, stop=True), inc="mmr")
                    p.wait("q", [slf4["gg"][cp], slf4["vvt"][cp]])
                    t_g = p.dma("q", slot["gg"][:, cp, :], g_d[q, t, c * 128:(c + 1) * 128, :], inc="ldx")
                    t_v = p.dma("q", slot[vslot][:, cp, :], v_d[q, t, c * 128:(c + 1) * 128, :], inc="ldx")
                    p.wait("v", [t_bmu, t_brs, t_b5, t_g, t_v])
                    p.emit("v", lambda v, c=c, t=t: v.tensor_tensor(
                        yv[:, c, :], yv[:, c, :], bc_ap("p0"), ALU.subtract), inc=None)
                    p.emit("v", lambda v, c=c, t=t: v.tensor_tensor(
                        yv[:, c, :], yv[:, c, :], bc_ap("p1"), ALU.mult), inc=None)
                    p.emit("v", lambda v, c=c, t=t: v.tensor_scalar(
                        yv[:, c, :], yv[:, c, :], cols["lnxw"][:, c:c + 1],
                        cols["lnxb"][:, c:c + 1], ALU.mult, ALU.add), inc=None)
                    p.emit("v", lambda v, c=c, cp=cp, vslot=vslot: v.tensor_tensor(
                        slot[vslot][:, cp, :], slot[vslot][:, cp, :], bc_ap("ab0"), ALU.mult), inc=None)
                    p.emit("v", lambda v, c=c, t=t, cp=cp, vslot=vslot: v.tensor_tensor(
                        yv[:, c, :], yv[:, c, :], slot[vslot][:, cp, :], ALU.add), inc=None)
                    t_z = p.emit("v", lambda v, c=c, t=t, cp=cp: v.tensor_tensor(
                        zbf[:, c, :], yv[:, c, :], slot["gg"][:, cp, :], ALU.mult), inc="vr")
                    p.wait("t", t_z)
                    slf4["gg"][cp] = t_z
                    slf4["vvt"][cp] = t_z
                p.barrier()

                p.wait("t", tok_w if (q == 0 and t == 0) else None)
                t_u = None
                for ec in range(NCH):
                    sl_ = "bb0" if ec % 2 == 0 else "bb1"
                    p.wait("t", t_u)
                    t_mm = None
                    for dk in range(NCH):
                        t_mm = p.emit("t", lambda tt, dk=dk, ec=ec, sl_=sl_: tt.matmul(
                            bc_ap(sl_), wo[:, dk, ec * 128:(ec + 1) * 128],
                            zbf[:, dk, :], start=(dk == 0), stop=(dk == NCH - 1)),
                            inc=("mmp" if dk == NCH - 1 else None))
                    p.wait("q", slxf[ec % 2])
                    t_lx = p.dma("q", slx[:, ec % 2, :],
                                 x2_d[q, t, ec * 128:(ec + 1) * 128, :], inc="ldx")
                    p.wait("v", [t_mm, t_lx])
                    t_u = p.emit("v", lambda v, ec=ec, sl_=sl_: v.tensor_tensor(
                        us[:, ec, :], slx[:, ec % 2, :], bc_ap(sl_), ALU.add), inc="vr")
                    slxf[ec % 2] = t_u
                p.barrier()

                u_aps = [us[:, c, :] for c in range(NCH)]
                tok_rows = emit_ln_rows(p, E, u_aps, t_u)
                t_h = None
                for c in range(NCH):
                    t_ha = emit_ln_apply_chunk(p, E, c, us[:, c, :], fs[:, c, :],
                                               tok_rows, None, None, None)
                    p.wait("v", t_ha)
                    t_h = p.emit("v", lambda v, c=c: v.tensor_scalar(
                        hbf[:, c, :], fs[:, c, :], cols["nrmw"][:, c:c + 1],
                        cols["nrmb"][:, c:c + 1], ALU.mult, ALU.add), inc="vr")
                p.barrier()

                t_f = None
                for ec in range(NCH):
                    sl_ = "bb0" if ec % 2 == 0 else "bb1"
                    cp = ec % 2
                    p.wait("t", t_f)
                    t_mm = None
                    for dk in range(NCH):
                        t_mm = p.emit("t", lambda tt, dk=dk, ec=ec, sl_=sl_: tt.matmul(
                            bc_ap(sl_), wf[:, dk, ec * 128:(ec + 1) * 128],
                            hbf[:, dk, :], start=(dk == 0), stop=(dk == NCH - 1)),
                            inc=("mmp" if dk == NCH - 1 else None))
                    p.wait("s", t_mm)
                    t_e1 = p.emit("s", lambda s, ec=ec, cp=cp, sl_=sl_: s.activation(
                        slotb["pr"][:, cp, :], bc_ap(sl_), AF.Exp, scale=-1.0,
                        bias=nbfc[:, ec:ec + 1]), inc="se")
                    p.wait("v", t_e1)
                    t_zz = p.emit("v", lambda v, ec=ec, cp=cp, sl_=sl_: v.tensor_scalar(
                        slot["k0"][:, cp, :], bc_ap(sl_), cols["bfc"][:, ec:ec + 1],
                        None, ALU.add), inc="vr")
                    t_pp = p.emit("v", lambda v, cp=cp: v.tensor_scalar(
                        slot["k1"][:, cp, :], slotb["pr"][:, cp, :], 1.0, None, ALU.add), inc="vr")
                    p.wait("s", t_pp)
                    p.emit("s", lambda s, cp=cp: s.activation(
                        slot["k1"][:, cp, :], slot["k1"][:, cp, :], AF.Ln), inc=None)
                    t_sg = p.emit("s", lambda s, cp=cp: s.activation(
                        slot["k1"][:, cp, :], slot["k1"][:, cp, :], AF.Exp, scale=-1.0), inc="se")
                    p.wait("v", [t_sg, t_zz])
                    t_f = p.emit("v", lambda v, ec=ec, cp=cp: v.tensor_tensor(
                        fs[:, ec, :], slot["k0"][:, cp, :], slot["k1"][:, cp, :], ALU.mult), inc="vr")
                    p.wait("t", t_f)
                    p.wait("s", t_f)
                p.barrier()

                f_aps = [fs[:, c, :] for c in range(NCH)]
                tok_rows = emit_ln_rows(p, E, f_aps, t_f)
                t_s = None
                for c in range(NCH):
                    t_oa = emit_ln_apply_chunk(p, E, c, fs[:, c, :], us[:, c, :],
                                               tok_rows, None, None, None)
                    p.wait("v", t_oa)
                    p.emit("v", lambda v, c=c: v.tensor_scalar(
                        us[:, c, :], us[:, c, :], cols["lnfw"][:, c:c + 1],
                        cols["lnfb"][:, c:c + 1], ALU.mult, ALU.add), inc=None)
                    p.wait("q", slxf[c % 2])
                    t_lx = p.dma("q", slx[:, c % 2, :],
                                 x2_d[q, t, c * 128:(c + 1) * 128, :], inc="ldx")
                    p.wait("v", t_lx)
                    t_s = p.emit("v", lambda v, c=c: v.tensor_tensor(
                        us[:, c, :], us[:, c, :], slx[:, c % 2, :], ALU.add), inc="vr")
                    slxf[c % 2] = t_s
                p.barrier()
                s_aps = [us[:, c, :] for c in range(NCH)]
                tok_rows = emit_ln_rows(p, E, s_aps, t_s)
                t_x3 = None
                for c in range(NCH):
                    t_x3 = emit_ln_apply_chunk(p, E, c, us[:, c, :], fs[:, c, :],
                                               tok_rows, None, None, None)
                    p.wait("q", t_x3)
                    t_sp = p.dma("q", x3_d[t * NQ + q, c * 128:(c + 1) * 128, :],
                                 fs[:, c, :], inc="stq")
                p.barrier()

        p.wait("q", t_sp)
        p.realize(es)
    return nc



def build_ld0():
    nc = bass.Bass("TRN2", target_bir_lowering=False)
    x3_d = nc.dram_tensor("x3", [NST, D, T], F32, kind="ExternalInput").ap()
    wrT_d = nc.dram_tensor("wrT", [D, 8], F32, kind="ExternalInput").ap()
    rbc_d = nc.dram_tensor("rbc", [8, 1], F32, kind="ExternalInput").ap()
    tbb_d = nc.dram_tensor("tbb", [128, 8], F32, kind="ExternalInput").ap()
    rbt_d = nc.dram_tensor("rbt", [128, 8], F32, kind="ExternalInput").ap()
    eye128_d = nc.dram_tensor("eye128", [128, 128], F32, kind="ExternalInput").ap()
    x3b_d = nc.dram_tensor("x3b_o", [NST, D, T], BF16, kind="ExternalOutput").ap()
    cw_d = nc.dram_tensor("cw_o", [NST, 8, T], BF16, kind="ExternalOutput").ap()

    p = Prog(nc)
    with ExitStack() as es:
        sb = lambda nm, shp, dt=F32: es.enter_context(nc.sbuf_tensor(nm, shp, dt))
        p.bscr = sb("bscr", [1, 4])
        x3s = sb("x3s", [128, NCH, T])
        x3b = sb("x3b_s", [128, NCH, T], BF16)
        wr = sb("wr_s", [128, NCH, 8], BF16)
        rbc = sb("rbc_s", [8, 1])
        tbb = sb("tbb_s", [128, 8])
        rbt = sb("rbt_s", [128, 8])
        eye = sb("eye_s", [128, 128])
        lsel = sb("lsel", [128, 8])
        m1 = sb("m1c", [128, 1])
        m2 = sb("m2c", [128, 1])
        f1 = sb("f1c", [128, 8])
        f2 = sb("f2c", [128, 8])
        s1c = sb("s1c", [128, 1])
        s2c = sb("s2c", [128, 1])
        cwtm = sb("cwtm", [128, 4, 8])
        cwfm = sb("cwfm", [8, T], BF16)
        ps_g = es.enter_context(nc.psum_tensor("ps_g", [128, 32], F32))
        ps_tp = es.enter_context(nc.psum_tensor("ps_tp", [8, 4, 128], F32))

        tok_gw = p.dma("g", wr[:], wrT_d.rearrange("(k p) e -> p k e", p=128), inc="ldw")
        p.dma("q", rbc[:], rbc_d, inc="ldq")
        p.dma("q", tbb[:], tbb_d, inc="ldq")
        p.dma("q", rbt[:], rbt_d, inc="ldq")
        tok_eye = p.dma("q", eye[:], eye128_d, inc="ldq")
        p.wait("v", tok_eye)
        p.wait("s", tok_eye)
        p.wait("t", [tok_gw, tok_eye])
        p.barrier()

        x3_free = [None]
        cw_free = [None]
        for st in range(NST):
            p.wait("q", x3_free[0])
            t_x = p.dma("q", x3s[:], x3_d[st].rearrange("(k p) t -> p k t", p=128), inc="ldx")
            p.wait("v", t_x)
            t_xb = None
            for c in range(NCH):
                t_xb = p.emit("v", lambda v, c=c: v.tensor_copy(x3b[:, c, :], x3s[:, c, :]), inc="vr")
            p.wait("q", t_xb)
            t_spb = p.dma("q", x3b_d[st].rearrange("(k p) t -> p k t", p=128), x3b[:], inc="stq")
            x3_free[0] = t_xb
            p.wait("t", t_xb)
            t_g = None
            for i in range(4):
                for dk in range(NCH):
                    t_g = p.emit("t", lambda t, i=i, dk=dk: t.matmul(
                        ps_g[:, 8 * i:8 * i + 8], x3b[:, dk, i * 128:(i + 1) * 128],
                        wr[:, dk, :], start=(dk == 0), stop=(dk == NCH - 1)),
                        inc="mmg")
            t_cw = None
            prev_s = None
            for i in range(4):
                p.wait("v", [t_g, cw_free[0]] if i == 0 else prev_s)
                t_e = p.emit("v", lambda v, i=i: v.tensor_tensor(
                    lsel[:], ps_g[:, 8 * i:8 * i + 8], tbb[:], ALU.add), inc="vg")
                p.wait("v", t_e)
                t_e = p.emit("v", lambda v: v.tensor_tensor(
                    lsel[:], lsel[:], rbt[:], ALU.add), inc="vg")
                p.wait("v", t_e)
                t_e = p.emit("v", lambda v: v.reduce_max(m1[:, :], lsel[:], axis=AX.X), inc="vg")
                p.wait("v", t_e)
                t_e = p.emit("v", lambda v: v.tensor_scalar(
                    f1[:], lsel[:], m1[:, :], None, ALU.is_equal), inc="vg")
                p.wait("v", t_e)
                t_e = p.emit("v", lambda v: v.scalar_tensor_tensor(
                    lsel[:], f1[:], -1.0e30, lsel[:], ALU.mult, ALU.add), inc="vg")
                p.wait("v", t_e)
                t_e = p.emit("v", lambda v: v.reduce_max(m2[:, :], lsel[:], axis=AX.X), inc="vg")
                p.wait("v", t_e)
                t_dd = p.emit("v", lambda v: v.tensor_scalar(
                    f2[:], lsel[:], m2[:, :], None, ALU.is_equal), inc="vg")
                p.wait("s", [t_dd, t_cw])
                p.emit("s", lambda s: s.activation(s1c[:, :], m1[:, :], AF.Sigmoid), inc=None)
                t_s1 = p.emit("s", lambda s: s.activation(s2c[:, :], m2[:, :], AF.Sigmoid), inc="sg")
                p.emit("s", lambda s: s.activation(p.bscr[0:1, 2:3], p.bscr[0:1, 0:1], AF.Copy), inc=None)
                t_s1b = p.emit("s", lambda s: s.activation(p.bscr[0:1, 2:3], p.bscr[0:1, 0:1], AF.Copy), inc="sg")
                prev_s = t_s1b
                p.wait("v", t_s1b)
                t_e = p.emit("v", lambda v, i=i: v.tensor_scalar(
                    cwtm[:, i, :], f1[:], s1c[:, :], None, ALU.mult), inc="vg")
                p.wait("v", t_e)
                t_e = p.emit("v", lambda v, i=i: v.tensor_scalar(
                    f2[:], f2[:], s2c[:, :], None, ALU.mult), inc="vg")
                p.wait("v", t_e)
                t_cw = p.emit("v", lambda v, i=i: v.tensor_tensor(
                    cwtm[:, i, :], cwtm[:, i, :], f2[:], ALU.add), inc="vg")
                p.wait("v", t_cw)
            p.wait("t", t_cw)
            t_tp = None
            for i in range(4):
                t_tp = p.emit("t", lambda t, i=i: t.transpose(
                    ps_tp[:, i, :], cwtm[:, i, :], eye[:]), inc="mmg")
            p.wait("v", t_tp)
            t_ev = p.emit("v", lambda v: v.tensor_copy(
                cwfm[:].rearrange("p (i t) -> p i t", i=4), ps_tp[:, :, :]), inc="vg")
            cw_free[0] = t_ev
            p.wait("q", t_ev)
            t_sp = p.dma("q", cw_d[st], cwfm[:], inc="stq")

        p.wait("q", t_sp)
        p.realize(es)
    return nc



def build_ld1():
    nc = bass.Bass("TRN2", target_bir_lowering=False)
    ICH = 16
    x3_d = nc.dram_tensor("x3", [NST, D, T], F32, kind="ExternalInput").ap()
    x3b_d = nc.dram_tensor("x3b", [NST, D, T], BF16, kind="ExternalInput").ap()
    cw_d = nc.dram_tensor("cw", [NST, 8, T], BF16, kind="ExternalInput").ap()
    wg_d = nc.dram_tensor("wg", [9, D, 2048], F32, kind="ExternalInput").ap()
    wu_d = nc.dram_tensor("wu", [9, D, 2048], F32, kind="ExternalInput").ap()
    wd_d = nc.dram_tensor("wd", [9, 2048, D], F32, kind="ExternalInput").ap()
    sel8_d = nc.dram_tensor("sel8", [8, 8, 128], F32, kind="ExternalInput").ap()
    y_d = nc.dram_tensor("y_o", [NST, D, T], F32, kind="ExternalOutput").ap()
    acc_d = nc.dram_tensor("acc_sc", [NST, D, T], F32, kind="Internal").ap()

    p = Prog(nc)
    with ExitStack() as es:
        sb = lambda nm, shp, dt=F32: es.enter_context(nc.sbuf_tensor(nm, shp, dt))
        p.bscr = sb("bscr", [1, 4])
        wg = sb("wg_s", [128, NCH, 2048], BF16)
        wu = sb("wu_s", [128, NCH, 2048], BF16)
        wd = sb("wd_s", [128, ICH, D], BF16)
        x3b = sb("x3b_s", [128, NCH, T], BF16)
        hbf = sb("hbf_s", [128, ICH, T], BF16)
        accs = sb("accs", [128, NCH, T])
        cwfm = sb("cwfm", [8, T], BF16)
        cwb = sb("cwb", [128, T])
        sel8 = sb("sel8_s", [8, 8, 128], BF16)
        sil = sb("sil_s", [128, 2, T])
        tmp = sb("tmp_s", [128, 2, T])
        E, tok_lnprep = alloc_ln_env(p, nc, es, sb)
        ps_p = es.enter_context(nc.psum_tensor("ps_p", [128, 2, T], F32))

        tok_sel = p.dma("g", sel8[:], sel8_d.rearrange("e j p -> j e p"), inc="ldw")

        w_free = [None]
        xn_free = [None]
        acc_free = [None]
        cw_free = [None]
        tmp_free = [None, None]
        sil_free = [None, None]
        hb_free = [None]
        pscw_free = [None]
        psp_free = [None, None]
        y_free = [None]
        t_sp = None

        for e in range(9):
            p.wait("g", w_free[0])
            t_w1 = p.dma("g", wg[:], wg_d[e].rearrange("(k p) h -> p k h", p=128), inc="ldw")
            t_wu = p.dma("g", wu[:], wu_d[e].rearrange("(k p) h -> p k h", p=128), inc="ldw")
            t_w2 = p.dma("g", wd[:], wd_d[e].rearrange("(k p) h -> p k h", p=128), inc="ldw")

            last_mm = None
            for st in range(NST):
                p.wait("q", xn_free[0])
                t_xn = p.dma("q", x3b[:], x3b_d[st].rearrange("(k p) t -> p k t", p=128), inc="ldx")
                p.wait("q", acc_free[0])
                src_acc = x3_d if e == 0 else acc_d
                t_acc = p.dma("q", accs[:], src_acc[st].rearrange("(k p) t -> p k t", p=128), inc="ldx")
                if e > 0:
                    p.wait("q", cw_free[0])
                    t_cwl = p.dma("q", cwfm[:], cw_d[st], inc="ldx")
                    p.wait("t", [tok_sel, t_cwl, pscw_free[0]])
                    t_cb = p.emit("t", lambda t, e=e: t.matmul(
                        E["ps_s1"][0:128, :] if False else E["ps_ab"][:, 0, :],
                        sel8[:, e - 1, :], cwfm[:, :], start=True, stop=True), inc="mmcb")
                    p.wait("v", t_cb)
                    t_cbe = p.emit("v", lambda v: v.tensor_copy(cwb[:], E["ps_ab"][:, 0, :]), inc="vc")
                    pscw_free[0] = t_cbe
                    cw_free[0] = t_cb
                else:
                    t_cbe = None

                p.wait("t", [t_w1, t_xn] if st == 0 else [t_xn])
                t_h = None
                for ic in range(ICH):
                    ep = ic % 2
                    p.wait("t", psp_free[ep])
                    t_mmg = None
                    for dk in range(NCH):
                        t_mmg = p.emit("t", lambda t, dk=dk, ic=ic, ep=ep: t.matmul(
                            ps_p[:, ep, :], wg[:, dk, ic * 128:(ic + 1) * 128],
                            x3b[:, dk, :], start=(dk == 0), stop=(dk == NCH - 1)),
                            inc=("mmp" if dk == NCH - 1 else None))
                    p.wait("s", [t_mmg, sil_free[ep]])
                    t_sil = p.emit("s", lambda s, ic=ic, ep=ep: s.activation(
                        sil[:, ep, :], ps_p[:, ep, :], AF.Silu), inc="se")
                    psp_free[ep] = t_sil
                    p.wait("t", [t_sil, hb_free[0] if ic == 0 else None,
                                 t_wu if (st == 0 and ic == 0) else None])
                    t_mmu = None
                    for dk in range(NCH):
                        t_mmu = p.emit("t", lambda t, dk=dk, ic=ic, ep=ep: t.matmul(
                            ps_p[:, ep, :], wu[:, dk, ic * 128:(ic + 1) * 128],
                            x3b[:, dk, :], start=(dk == 0), stop=(dk == NCH - 1)),
                            inc=("mmp" if dk == NCH - 1 else None))
                    p.wait("v", t_mmu)
                    t_h = p.emit("v", lambda v, ic=ic, ep=ep: v.tensor_tensor(
                        hbf[:, ic, :], sil[:, ep, :], ps_p[:, ep, :], ALU.mult), inc="vc")
                    psp_free[ep] = t_h
                    sil_free[ep] = t_h

                p.wait("t", [t_w2, t_h] if st == 0 else [t_h])
                acc_toks = []
                for ec in range(NCH):
                    ep = ec % 2
                    p.wait("t", psp_free[ep])
                    t_mm = None
                    for ic in range(ICH):
                        t_mm = p.emit("t", lambda t, ic=ic, ec=ec, ep=ep: t.matmul(
                            ps_p[:, ep, :], wd[:, ic, ec * 128:(ec + 1) * 128],
                            hbf[:, ic, :], start=(ic == 0), stop=(ic == ICH - 1)),
                            inc=("mmp" if ic == ICH - 1 else None))
                    last_mm = t_mm
                    cp = ec % 2
                    p.wait("v", [t_mm, t_acc, tmp_free[cp], t_cbe])
                    if e == 0:
                        t_ac = p.emit("v", lambda v, ec=ec, ep=ep: v.tensor_tensor(
                            accs[:, ec, :], accs[:, ec, :], ps_p[:, ep, :], ALU.add), inc="vc")
                    else:
                        t_t = p.emit("v", lambda v, ec=ec, ep=ep, cp=cp: v.tensor_tensor(
                            tmp[:, cp, :], cwb[:], ps_p[:, ep, :], ALU.mult), inc="vc")
                        t_ac = p.emit("v", lambda v, ec=ec, cp=cp: v.tensor_tensor(
                            accs[:, ec, :], accs[:, ec, :], tmp[:, cp, :], ALU.add), inc="vc")
                        tmp_free[cp] = t_ac
                    psp_free[ep] = t_ac
                    acc_toks.append(t_ac)
                    if e < 8:
                        p.wait("q", t_ac)
                        t_sp = p.dma("q", acc_d[st, ec * 128:(ec + 1) * 128, :],
                                     accs[:, ec, :], inc="stq")
                xn_free[0] = t_mm
                hb_free[0] = last_mm
                if e < 8:
                    acc_free[0] = [acc_toks[-1], t_sp]
                else:
                    p.barrier()
                    a_aps = [accs[:, c, :] for c in range(NCH)]
                    tok_rows = emit_ln_rows(p, E, a_aps, acc_toks[-1])
                    p.wait("v", tok_lnprep if st == 0 else None)
                    t_y = None
                    for c in range(NCH):
                        t_y = emit_ln_apply_chunk(
                            p, E, c, accs[:, c, :], accs[:, c, :],
                            tok_rows, None, None, y_free[0] if c == 0 else None)
                        p.wait("q", t_y)
                        t_sp = p.dma("q", y_d[st, c * 128:(c + 1) * 128, :],
                                     accs[:, c, :], inc="stq")
                    y_free[0] = t_y
                    acc_free[0] = [t_y, t_sp]
                    p.barrier()
            w_free[0] = last_mm

        p.wait("q", t_sp)
        p.realize(es)
    return nc


def ld_in_maps(inp, lc2_res, ld0_res=None):
    wrT = ctr_(inp["L1_ffn_Wr"])
    rbc = np.ascontiguousarray(np.asarray(inp["L1_ffn_rb"], np.float32).reshape(8, 1))
    rbt = np.ascontiguousarray(np.tile(np.asarray(inp["L1_ffn_rb"], np.float32).reshape(1, 8), (128, 1)))
    tbb = np.ascontiguousarray(np.tile((np.arange(8)[::-1].astype(np.float32) * 1e-6).reshape(1, 8), (128, 1)))
    eye128 = np.eye(128, dtype=np.float32)
    wg = np.concatenate([ctr_(inp["L1_ffn_Wg_s"])[None],
                         np.stack([ctr_(inp["L1_ffn_Wg"][e]) for e in range(8)])])
    wu = np.concatenate([ctr_(inp["L1_ffn_Wu_s"])[None],
                         np.stack([ctr_(inp["L1_ffn_Wu"][e]) for e in range(8)])])
    wd = np.concatenate([ctr_(inp["L1_ffn_Wd_s"])[None],
                         np.stack([ctr_(inp["L1_ffn_Wd"][e]) for e in range(8)])])
    sel8 = np.zeros((8, 8, 128), np.float32)
    for e in range(8):
        sel8[e, e, :] = 1.0
    maps0, maps1 = [], []
    for c in range(NCORE):
        x3 = lc2_res[c]["x3_o"]
        maps0.append(dict(x3=x3, wrT=wrT, rbc=rbc, tbb=tbb, rbt=rbt, eye128=eye128))
        if ld0_res is not None:
            r0 = ld0_res[c]
            maps1.append(dict(x3=x3, x3b=r0["x3b_o"], cw=r0["cw_o"],
                              wg=wg, wu=wu, wd=wd, sel8=sel8))
    return maps0, maps1


def run_ld(inp, lc2_res):
    nc0 = _get_nc("ld0", build_ld0)
    maps0, _ = ld_in_maps(inp, lc2_res)
    res0 = _run_spmd(nc0, maps0).results
    nc1 = _get_nc("ld1", build_ld1)
    _, maps1 = ld_in_maps(inp, lc2_res, res0)
    res1 = _run_spmd(nc1, maps1).results
    return res0, res1


def lc_in_maps(inp, lb1_res=None, lc1_res=None):
    w6 = np.stack([np.ascontiguousarray(np.asarray(inp["L1_attn_W" + m], np.float32))
                   for m in ["r", "w", "k", "v", "a", "g"]])
    mu6 = np.stack([col_(inp["L1_attn_mu_" + m]) for m in ["r", "w", "k", "v", "a", "g"]])
    w0c = col_(inp["L1_attn_w0"])
    a0c = col_(inp["L1_attn_a0"])
    wo = np.ascontiguousarray(np.asarray(inp["L1_attn_Wo"], np.float32))
    wf = np.ascontiguousarray(np.asarray(inp["L1_attn_Wf"], np.float32))
    hpat = np.zeros((8, 128, 16), np.float32)
    hpatT = np.zeros((8, 16, 128), np.float32)
    for c in range(8):
        for pp in range(128):
            j = 2 * c + (1 if pp >= 64 else 0)
            hpat[c, pp, j] = 1.0
            hpatT[c, j, pp] = 1.0
    colmap = dict(kkc=col_(inp["L1_attn_k_k"]), kac=col_(inp["L1_attn_k_a"]),
                  rkc=col_(inp["L1_attn_r_k"]), lnxw=col_(inp["L1_attn_lnx_w"]),
                  lnxb=col_(inp["L1_attn_lnx_b"]), nrmw=col_(inp["L1_attn_norm_w"]),
                  nrmb=col_(inp["L1_attn_norm_b"]), bfc=col_(inp["L1_attn_bf"]),
                  lnfw=col_(inp["L1_attn_lnf_w"]), lnfb=col_(inp["L1_attn_lnf_b"]))
    maps1, maps2 = [], []
    for c in range(NCORE):
        if lb1_res is not None:
            maps1.append(dict(y1=lb1_res[c]["y1_o"], w6=w6, mu6=mu6, w0c=w0c, a0c=a0c))
        if lc1_res is not None:
            r1 = lc1_res[c]
            maps2.append(dict(x2=r1["x2_o"], r_i=r1["r_o"], w1_i=r1["w1_o"],
                              k_i=r1["k_o"], v_i=r1["v_o"], a_i=r1["a_o"], g_i=r1["g_o"],
                              wo=wo, wf=wf, hpat=hpat, hpatT=hpatT, **colmap))
    return maps1, maps2


def run_lc(inp, lb1_res):
    nc1 = _get_nc("lc1", build_lc1)
    maps1, _ = lc_in_maps(inp, lb1_res)
    res1 = _run_spmd(nc1, maps1).results
    nc2 = _get_nc("lc2", build_lc2)
    _, maps2 = lc_in_maps(inp, None, res1)
    res2 = _run_spmd(nc2, maps2).results
    return res1, res2



def la1_in_maps(inp):
    x = np.asarray(inp["L0_attn_x"], dtype=np.float32)
    wT = np.stack([ctr_(inp["L0_attn_Wr"]), ctr_(inp["L0_attn_Ww"]),
                   ctr_(inp["L0_attn_Wk"]), ctr_(inp["L0_attn_Wv"])])
    g_row = row_(inp["L0_attn_ln1_g"])
    b_row = row_(inp["L0_attn_ln1_b"])
    tdc = col_(inp["L0_attn_time_decay"])
    td4 = np.ascontiguousarray(np.tile(tdc, (1, 4)))
    maps = []
    for c in range(NCORE):
        sl = slice(c * SEQ, (c + 1) * SEQ)
        xc = np.ascontiguousarray(np.transpose(x[:, sl, :], (0, 2, 1)))
        esct = np.zeros((8, 128, 32), np.float32)
        maskt = np.zeros((8, 128, 32), np.float32)
        for j in range(c):
            esct[j] = 2048.0 * (c - 1 - j)
            maskt[j] = 1.0
        maps.append(dict(x=xc, wT=wT, ln1g_row=g_row, ln1b_row=b_row,
                         td_col=tdc, td4=td4, esct=esct, maskt=maskt))
    return maps


_NC_CACHE = {}


def _run_spmd(nc, maps):
    import time
    try:
        return run_bass_kernel_spmd(nc, maps, core_ids=list(range(NCORE)))
    except Exception:
        time.sleep(5.0)
        return run_bass_kernel_spmd(nc, maps, core_ids=list(range(NCORE)))


def _get_nc(name, builder):
    if name not in _NC_CACHE:
        _NC_CACHE[name] = builder()
    return _NC_CACHE[name]


def run_la1(inp):
    nc = _get_nc("la1", build_la1)
    maps = la1_in_maps(inp)
    res = _run_spmd(nc, maps)
    return res.results


def kernel(**inputs):
    la1 = run_la1(inputs)
    la2 = run_la2(inputs, la1)
    la3 = run_la3(inputs, la2)
    lb0, lb1 = run_lb(inputs, la3)
    lc1, lc2 = run_lc(inputs, lb1)
    ld0, ld1 = run_ld(inputs, lc2)
    out = np.empty((2, 16384, D), dtype=np.float32)
    for c in range(NCORE):
        y = ld1[c]["y_o"]
        for b in range(2):
            for q in range(NQ):
                sl = slice(c * SEQ + q * T, c * SEQ + (q + 1) * T)
                out[b, sl, :] = y[b * NQ + q].T
    return out
